# version 16
# baseline (speedup 1.0000x reference)
import numpy as np
import ml_dtypes
from contextlib import ExitStack

import concourse.bass as bass
import concourse.mybir as mybir
from concourse.bass_utils import run_bass_kernel_spmd

F32 = mybir.dt.float32
BF16 = mybir.dt.bfloat16
AF = mybir.ActivationFunctionType
ALU = mybir.AluOpType
AX = mybir.AxisListType

P = 128
S = 4096
D = 2048
NCH = S // P
NT = 8
NE = 18
NG = NE // 2
EPS = 1e-5
BF = ml_dtypes.bfloat16
NUPB = 3


class Sched:
    def __init__(self):
        self.cnt = {}
        self.ops = {k: [] for k in "tvsgy"}

    def emit(self, eng, fn, waits=(), inc=None, n=1):
        w = tuple((s, v) for (s, v) in waits if v and v > 0)
        if inc is not None:
            self.cnt[inc] = self.cnt.get(inc, 0) + n
        self.ops[eng].append((w, fn, inc, n))
        return self.cnt.get(inc, 0)

    def now(self, name):
        return self.cnt.get(name, 0)

    def replay(self, key, eng, sems):
        for w, fn, inc, n in self.ops[key]:
            for s_name, v in w:
                eng.wait_ge(sems[s_name], v)
            inst = fn(eng)
            if inc is not None:
                inst.then_inc(sems[inc], n)


def build_nc():
    nc = bass.Bass("TRN2", target_bir_lowering=False, debug=False)

    x_h = nc.dram_tensor("x", [S, D], F32, kind="ExternalInput").ap()
    wq_h = nc.dram_tensor("wq", [D, D], F32, kind="ExternalInput").ap()
    wkv_h = nc.dram_tensor("wkv", [D, 2 * D], F32, kind="ExternalInput").ap()
    bq_row_h = nc.dram_tensor("bq_row", [1, D], F32, kind="ExternalInput").ap()
    bkv_row_h = nc.dram_tensor("bkv_row", [1, 2 * D], F32, kind="ExternalInput").ap()
    gam_h = nc.dram_tensor("gam_row", [1, D], F32, kind="ExternalInput").ap()
    bet_h = nc.dram_tensor("bet_row", [1, D], F32, kind="ExternalInput").ap()
    bout_h = nc.dram_tensor("bout_row", [1, D], F32, kind="ExternalInput").ap()
    wout_h = nc.dram_tensor("wout", [D, D], F32, kind="ExternalInput").ap()
    wrt_h = nc.dram_tensor("wrt", [D, 16], F32, kind="ExternalInput").ap()
    wup_h = nc.dram_tensor("wup", [NE, D, 1024], F32, kind="ExternalInput").ap()
    wdn_h = nc.dram_tensor("wdn", [NE, 1024, D], F32, kind="ExternalInput").ap()
    cs_h = nc.dram_tensor("cs", [S, 1024], BF16, kind="ExternalInput").ap()
    idf_h = nc.dram_tensor("idf", [P, P], F32, kind="ExternalInput").ap()
    idb_h = nc.dram_tensor("idb", [P, P], BF16, kind="ExternalInput").ap()
    sel_h = nc.dram_tensor("sel", [16, 16 * P], BF16, kind="ExternalInput").ap()
    y_h = nc.dram_tensor("y", [S, D], F32, kind="ExternalOutput").ap()

    wupbf = nc.dram_tensor("wupbf", [NE, 16, P, 1024], BF16, kind="Internal").ap()
    wdnbf = nc.dram_tensor("wdnbf", [NE, 8, P, D], BF16, kind="Internal").ap()
    wqbf = nc.dram_tensor("wqbf", [16, P, D], BF16, kind="Internal").ap()
    woabf = nc.dram_tensor("woabf", [16, P, D], BF16, kind="Internal").ap()
    s0h = nc.dram_tensor("s0h", [NCH, P, D], BF16, kind="Internal").ap()
    xnTh = nc.dram_tensor("xnTh", [16, P, S], BF16, kind="Internal").ap()

    sch = Sched()
    E = sch.emit
    now = sch.now

    with ExitStack() as top:
        en = top.enter_context
        x_sb = [en(nc.sbuf_tensor(f"x{i}", [P, D], F32)) for i in range(2)]
        gate_all = en(nc.sbuf_tensor("gate_all", [P, NCH * 16], F32))
        A_col = en(nc.sbuf_tensor("A_col", [P, 16], F32))
        gam_b = en(nc.sbuf_tensor("gam_b", [P, D], BF16))
        bet_b = en(nc.sbuf_tensor("bet_b", [P, D], BF16))
        idf = en(nc.sbuf_tensor("idf_sb", [P, P], F32))
        idb = en(nc.sbuf_tensor("idb_sb", [P, P], BF16))
        ones_f = en(nc.sbuf_tensor("ones_f", [1, P], F32))
        ones_c = en(nc.sbuf_tensor("ones_c", [P, 1], F32))
        eps_t = en(nc.sbuf_tensor("eps_t", [P, 1], F32))
        eps2_t = en(nc.sbuf_tensor("eps2_t", [P, 1], F32))
        st6 = en(nc.sbuf_tensor("st6", [P, 24], F32))
        mv = en(nc.sbuf_tensor("mv", [P, 4], F32))
        rst = en(nc.sbuf_tensor("rst", [P, 4], F32))
        lt = en(nc.sbuf_tensor("lt", [P, 2], F32))
        sq = en(nc.sbuf_tensor("sq", [P, 2], F32))
        rkv = en(nc.sbuf_tensor("rkv", [P, 1], F32))
        wrt_sb = en(nc.sbuf_tensor("wrt_sb", [P, 16 * 16], F32))
        ones_b128 = en(nc.sbuf_tensor("ones_b128", [1, P], BF16))
        probs = en(nc.sbuf_tensor("probs", [P, 16], F32))
        negm = en(nc.sbuf_tensor("negm", [P, 1], F32))
        ssum = en(nc.sbuf_tensor("ssum", [P, 2], F32))
        m8 = en(nc.sbuf_tensor("m8", [P, 8], F32))
        msk = en(nc.sbuf_tensor("msk", [P, 16], F32))
        bar_t = en(nc.sbuf_tensor("bar_t", [1, 8], F32))
        mu_all = en(nc.sbuf_tensor("mu_all", [P, NCH], F32))
        rstd_all = en(nc.sbuf_tensor("rstd_all", [P, NCH], F32))

        sem_names = ["sX", "sW", "sW2", "sG", "sPE", "sDV", "sAC", "sPL",
                     "sST", "sOUT", "sBAR"]
        sems = {n_: en(nc.semaphore(n_)) for n_ in sem_names}

        def barrier():
            E('t', lambda e: e.nop(), inc='sBAR')
            E('v', lambda e: e.memset(bar_t[0:1, 0:1], 0.0), inc='sBAR')
            E('s', lambda e: e.activation(out=bar_t[0:1, 1:2], in_=bar_t[0:1, 1:2],
                                          func=AF.Copy, scale=1.0), inc='sBAR')
            return now('sBAR')

        w_cst = 0
        for fn in [
            lambda e: e.dma_start(out=idf[:], in_=idf_h),
            lambda e: e.dma_start(out=idb[:], in_=idb_h),
            lambda e: e.dma_start(out=wrt_sb[:].rearrange("p (k n) -> p k n", k=16),
                                  in_=wrt_h.rearrange("(k p) n -> p k n", p=P)),
        ]:
            w_cst = E('y', fn, inc='sW', n=16)

        def memsets(v):
            v.memset(ones_f[:], 1.0)
            v.memset(ones_c[:], 1.0)
            v.memset(ones_b128[:], 1.0)
            v.memset(eps_t[:], EPS)
            return v.memset(eps2_t[:], EPS * EPS)
        dv_ones = E('v', memsets, inc='sDV')

        with ExitStack() as ph01:
            e1 = ph01.enter_context
            bkv_b = e1(nc.sbuf_tensor("bkv_b", [P, 2 * D], BF16))
            pe_bc_last = 0
            with ExitStack() as rows:
                er = rows.enter_context
                gam_row = er(nc.sbuf_tensor("gam_row_sb", [1, D], F32))
                bet_row = er(nc.sbuf_tensor("bet_row_sb", [1, D], F32))
                bkv_row = er(nc.sbuf_tensor("bkv_row_sb", [1, 2 * D], F32))
                bout_row = er(nc.sbuf_tensor("bout_row_sb", [1, D], F32))
                ps0 = [er(nc.psum_tensor(f"ps0_{i}", [P, 512], F32)) for i in range(2)]
                w_rows = 0
                for dst, src in [(gam_row, gam_h), (bet_row, bet_h),
                                 (bkv_row, bkv_row_h), (bout_row, bout_h)]:
                    w_rows = E('y', (lambda d, s_: lambda e: e.dma_start(
                        out=d[:], in_=s_))(dst, src), inc='sW', n=16)
                bc_jobs = (
                    [(gam_b[:, n * 512:(n + 1) * 512], gam_row[0:1, n * 512:(n + 1) * 512]) for n in range(4)]
                    + [(bet_b[:, n * 512:(n + 1) * 512], bet_row[0:1, n * 512:(n + 1) * 512]) for n in range(4)]
                    + [(bkv_b[:, n * 512:(n + 1) * 512],
                        bkv_row[0:1, n * 512:(n + 1) * 512]) for n in range(8)]
                )
                dv_bc = [0] * len(bc_jobs)
                for j, (dst, src) in enumerate(bc_jobs):
                    w = [("sW", w_rows), ("sDV", dv_ones)]
                    if j >= 2:
                        w.append(("sDV", dv_bc[j - 2]))
                    pe_j = E('t', (lambda s_, b: lambda e: e.matmul(
                        ps0[b][:], ones_f[:], s_, start=True, stop=True))(src, j % 2),
                        waits=w, inc='sPE')
                    dv_bc[j] = E('v', (lambda d, b: lambda e: e.tensor_copy(
                        out=d, in_=ps0[b][:]))(dst, j % 2),
                        waits=[("sPE", pe_j)], inc='sDV')
                    pe_bc_last = pe_j

            wkv_sb = e1(nc.sbuf_tensor("wkv_sb", [P, 16 * 2 * D], BF16))
            cs_sb = [e1(nc.sbuf_tensor(f"cs{i}", [P, 1024], BF16)) for i in range(2)]
            xn2b = e1(nc.sbuf_tensor("xn2b", [P, D], BF16))
            xn2T = e1(nc.sbuf_tensor("xn2T", [P, D], BF16))
            K_sb = e1(nc.sbuf_tensor("K_sb", [P, D], BF16))
            V_sb = e1(nc.sbuf_tensor("V_sb", [P, D], BF16))
            acc = e1(nc.sbuf_tensor("acc_ph1", [P, D], F32))
            dump = e1(nc.sbuf_tensor("dump", [P, D], F32))
            sqdump = e1(nc.sbuf_tensor("sqdump", [P, D], BF16))
            kr1, kr2 = dump[:, 0:512], dump[:, 512:1024]
            A_row = dump

            ps_t = e1(nc.psum_tensor("ps_t", [P, D], BF16))
            ps_kv = [e1(nc.psum_tensor(f"ps_kv{i}", [P, 512], F32)) for i in range(2)]
            ps_a = e1(nc.psum_tensor("ps_a", [1, 512], F32))
            ps_ac = e1(nc.psum_tensor("ps_ac", [P, 16], F32))

            g_wkv = E('g', lambda e: e.dma_start(
                out=wkv_sb[:].rearrange("p (k n) -> p k n", k=16),
                in_=wkv_h.rearrange("(k p) n -> p k n", p=P)),
                waits=[("sPE", pe_bc_last)], inc='sG', n=16)
            conv_jobs = []
            wq_r = wq_h.rearrange("(k p) n -> k p n", p=P)
            woa_r = wout_h.rearrange("(k p) n -> k p n", p=P)
            for q in range(4):
                conv_jobs.append((wqbf[4 * q:4 * (q + 1)], wq_r[4 * q:4 * (q + 1)]))
            for q in range(4):
                conv_jobs.append((woabf[4 * q:4 * (q + 1)], woa_r[4 * q:4 * (q + 1)]))
            up_last, dn_last = {}, {}
            for ee in range(NE):
                ur = wup_h[ee].rearrange("(k p) m -> k p m", p=P)
                dr = wdn_h[ee].rearrange("(k p) m -> k p m", p=P)
                for q in range(4):
                    conv_jobs.append((wupbf[ee][4 * q:4 * (q + 1)], ur[4 * q:4 * (q + 1)]))
                    up_last[ee] = len(conv_jobs) - 1
                for q in range(2):
                    conv_jobs.append((wdnbf[ee][4 * q:4 * (q + 1)], dr[4 * q:4 * (q + 1)]))
                    dn_last[ee] = len(conv_jobs) - 1
            njobs = len(conv_jobs)
            g_conv = [0] * njobs
            for jix, (dst, src) in enumerate(conv_jobs):
                thr = 16 * int(4 + jix * (88 - 4) / max(1, njobs - 1)) * 1
                g_conv[jix] = E('g', (lambda d, s_: lambda e: e.dma_start(out=d, in_=s_))(dst, src),
                                waits=[("sX", 16 * min(88, 4 + (jix * 84) // max(1, njobs - 1)))],
                                inc='sG', n=16)
            g_wq = g_conv[3]
            g_woa = g_conv[7]
            g_wup = [g_conv[up_last[ee]] for ee in range(NE)]
            g_wdn = [g_conv[dn_last[ee]] for ee in range(NE)]

            dv_xc = [0] * NCH
            dv_xn2 = [0] * NCH
            dv_krot = [0] * NCH
            pl_vrot = [0] * NCH
            dv_x2ev = [0] * NCH
            pe_tr = [0] * NCH
            dv_P = [0] * NCH
            dv_ev = [[0] * 8 for _ in range(NCH)]
            ac_rkv = [0] * NCH
            ac_rstd = [0] * NCH
            ac_sqv = [0] * NCH
            sx_x = [0] * NCH
            sx_cs = [0] * NCH

            def emit_xload(i):
                b = i % 2
                wx = [("sDV", dv_xc[i - 2]), ("sDV", pl_vrot[i - 2])] if i >= 2 \
                    else [("sPE", pe_bc_last)]
                sx_x[i] = E('y', (lambda i_, b_: lambda e: e.dma_start(
                    out=x_sb[b_][:], in_=x_h[i_ * P:(i_ + 1) * P, :]))(i, b),
                    waits=wx, inc='sX', n=16)
                wc = ([("sDV", pl_vrot[i - 2])] if i >= 2
                      else [("sPE", pe_bc_last)])
                sx_cs[i] = E('y', (lambda i_, b_: lambda e: e.dma_start(
                    out=cs_sb[b_][:], in_=cs_h[i_ * P:(i_ + 1) * P, :]))(i, b),
                    waits=wc, inc='sX', n=16)

            def emit_LN_stats(i):
                b = i % 2
                var = mv[:, 2 * b + 1: 2 * b + 2]
                rstdc = rst[:, 2 * b + 1: 2 * b + 2]
                E('v', (lambda b_: lambda e: e.bn_stats(
                    out=st6[:, 0:6], in_=x_sb[b_][:, 0:512]))(b),
                    waits=[("sX", sx_x[i])])
                dv_bns = 0
                for q in range(1, 4):
                    dv_bns = E('v', (lambda b_, q_: lambda e: e.bn_stats(
                        out=st6[:, 6 * q_:6 * (q_ + 1)],
                        in_=x_sb[b_][:, 512 * q_:512 * (q_ + 1)]))(b, q),
                        inc='sDV' if q == 3 else None)
                wagg = [("sDV", dv_bns)]
                if i >= 2:
                    wagg.append(("sAC", ac_rstd[i - 2]))
                dv_mv = E('v', (lambda b_: lambda e: e.bn_aggr(
                    out=mv[:, 2 * b_: 2 * b_ + 2], in_=st6[:]))(b),
                    waits=wagg, inc='sDV')
                ac_ln = E('s', (lambda var_: lambda e: e.activation(
                    out=lt[:, 1:2], in_=var_, func=AF.Ln,
                    scale=1.0 + EPS, bias=eps2_t[:]))(var),
                    waits=[("sDV", dv_mv)], inc='sAC')
                ac_rstd[i] = E('s', (lambda rc: lambda e: e.activation(
                    out=rc, in_=lt[:, 1:2], func=AF.Exp, scale=-0.5))(rstdc),
                    waits=[("sAC", ac_ln)], inc='sAC')

            def emit_LN_stt(i):
                b = i % 2
                mu = mv[:, 2 * b: 2 * b + 1]
                rstdc = rst[:, 2 * b + 1: 2 * b + 2]
                dv_xc[i] = E('v', (lambda b_, mu_: lambda e: e.scalar_tensor_tensor(
                    out=dump[:], in0=x_sb[b_][:], scalar=mu_, in1=gam_b[:],
                    op0=ALU.subtract, op1=ALU.mult))(b, mu),
                    waits=[("sAC", ac_rstd[i])], inc='sDV')
                wx2 = []
                if i >= 1:
                    wx2.append(("sPE", pe_tr[i - 1]))
                dv_xn2[i] = E('v', (lambda rc: lambda e: e.scalar_tensor_tensor(
                    out=xn2b[:], in0=dump[:], scalar=rc, in1=bet_b[:],
                    op0=ALU.mult, op1=ALU.add))(rstdc), waits=wx2, inc='sDV')

            pend_P = [None]
            emit_xload(0)
            emit_xload(1)
            emit_LN_stats(0)
            emit_LN_stt(0)

            def emit_trans(i):
                def trf(e):
                    r = None
                    for fo in range(16):
                        r = e.transpose(ps_t[:, fo * P:(fo + 1) * P],
                                        xn2b[:, fo * P:(fo + 1) * P], idb[:])
                    return r
                wtr = [("sDV", dv_xn2[i])]
                if i >= 1:
                    wtr.append(("sDV", dv_x2ev[i - 1]))
                pe_tr[i] = E('t', trf, waits=wtr, inc='sPE')
                dv_x2ev[i] = E('v', lambda e: e.tensor_copy(out=xn2T[:], in_=ps_t[:]),
                               waits=[("sPE", pe_tr[i])], inc='sDV')

            emit_trans(0)
            for i in range(NCH):
                b = i % 2
                if i + 1 < NCH:
                    emit_LN_stats(i + 1)

                ev_dst = [K_sb[:, 0:512], K_sb[:, 512:1024], K_sb[:, 1024:1536], K_sb[:, 1536:2048],
                          V_sb[:, 0:512], V_sb[:, 512:1024], V_sb[:, 1024:1536], V_sb[:, 1536:2048]]

                def kvmm(sl_):
                    def f(e):
                        r = None
                        for kc in range(16):
                            r = e.matmul(ps_kv[sl_ % 2][:],
                                         xn2T[:, kc * P:(kc + 1) * P],
                                         wkv_sb[:, kc * (2 * D) + sl_ * 512: kc * (2 * D) + (sl_ + 1) * 512],
                                         start=(kc == 0), stop=(kc == 15))
                        return r
                    return f

                def emit_kv(sl):
                    w = [("sDV", dv_x2ev[i])]
                    if i == 0 and sl == 0:
                        w.append(("sG", g_wkv))
                    if sl >= 2:
                        w.append(("sDV", dv_ev[i][sl - 2]))
                    elif i >= 1:
                        w.append(("sDV", dv_ev[i - 1][sl + 6]))
                    pe_sl = E('t', kvmm(sl), waits=w, inc='sPE')
                    wev_ = [("sPE", pe_sl)]
                    if i >= 1 and sl == 0:
                        wev_.append(("sDV", dv_P[i - 1]))
                    dv_ev[i][sl] = E('v', (lambda sl_, d: lambda e: e.tensor_add(
                        out=d, in0=ps_kv[sl_ % 2][:], in1=bkv_b[:, sl_ * 512:(sl_ + 1) * 512]))(sl, ev_dst[sl]),
                        waits=wev_, inc='sDV')

                emit_kv(0)
                emit_kv(1)

                def krot(b_):
                    def f(e):
                        cos, sin = cs_sb[b_][:, 0:512], cs_sb[b_][:, 512:1024]
                        ke, ko = K_sb[:, 0:512], K_sb[:, 512:1024]
                        e.tensor_mul(out=kr1, in0=ke, in1=cos)
                        e.tensor_mul(out=kr2, in0=ke, in1=sin)
                        e.tensor_mul(out=ke, in0=ko, in1=sin)
                        e.tensor_mul(out=ko, in0=ko, in1=cos)
                        e.tensor_sub(out=ke, in0=kr1, in1=ke)
                        return e.tensor_add(out=ko, in0=ko, in1=kr2)
                    return f
                dv_krot[i] = E('v', krot(b), waits=[("sX", sx_cs[i])], inc='sDV')
                if i + 1 < NCH:
                    emit_LN_stt(i + 1)
                for sl in range(2, 8):
                    emit_kv(sl)
                if i + 1 < NCH:
                    emit_trans(i + 1)

                def vrot(b_):
                    def f(e):
                        cos, sin = cs_sb[b_][:, 0:512], cs_sb[b_][:, 512:1024]
                        ve, vo = V_sb[:, 0:512], V_sb[:, 512:1024]
                        t1 = x_sb[b_][:, 0:512]
                        t2 = x_sb[b_][:, 512:1024]
                        e.tensor_mul(out=t1, in0=ve, in1=cos)
                        e.tensor_mul(out=t2, in0=ve, in1=sin)
                        e.tensor_mul(out=ve, in0=vo, in1=sin)
                        e.tensor_mul(out=vo, in0=vo, in1=cos)
                        e.tensor_sub(out=ve, in0=t1, in1=ve)
                        return e.tensor_add(out=vo, in0=vo, in1=t2)
                    return f
                wv = [("sX", sx_cs[i]), ("sDV", dv_ev[i][5]), ("sDV", dv_xc[i])]
                pl_vrot[i] = E('v', vrot(b), waits=wv, inc='sDV')

                ac1 = E('s', lambda e: e.activation(out=sqdump[:], in_=K_sb[:], func=AF.Square,
                                                    accum_out=sq[:, 0:1]),
                        waits=[("sDV", dv_krot[i]), ("sDV", dv_ev[i][3])], inc='sAC')
                ac_sqv[i] = E('s', lambda e: e.activation(out=sqdump[:], in_=V_sb[:], func=AF.Square,
                                                          accum_out=sq[:, 1:2]),
                              waits=[("sDV", dv_ev[i][7]), ("sDV", pl_vrot[i]), ("sAC", ac1)],
                              inc='sAC')
                ac3 = E('s', lambda e: e.activation(out=lt[:, 0:1], in_=sq[:, 0:1], func=AF.Ln),
                        waits=[("sAC", ac_sqv[i])], inc='sAC')
                ac4 = E('s', lambda e: e.activation(out=lt[:, 1:2], in_=sq[:, 1:2], func=AF.Ln),
                        waits=[("sAC", ac3)], inc='sAC')
                ac5 = E('s', lambda e: e.activation(out=lt[:, 1:2], in_=lt[:, 1:2], func=AF.Copy,
                                                    scale=-0.5),
                        waits=[("sAC", ac4)], inc='sAC')
                ac_rkv[i] = E('s', lambda e: e.activation(out=rkv[:], in_=lt[:, 0:1], func=AF.Exp,
                                                          scale=-0.5, bias=lt[:, 1:2]),
                              waits=[("sAC", ac5)], inc='sAC')
                ac_ksc = E('s', lambda e: e.activation(out=K_sb[:], in_=K_sb[:], func=AF.Copy,
                                                       scale=rkv[:]),
                           waits=[("sAC", ac_rkv[i])], inc='sAC')

                def pacc(i_, b_):
                    def f(e):
                        if i_ == 0:
                            return e.tensor_mul(out=acc[:], in0=K_sb[:], in1=V_sb[:])
                        e.tensor_mul(out=sqdump[:], in0=K_sb[:], in1=V_sb[:])
                        return e.tensor_add(out=acc[:], in0=acc[:], in1=sqdump[:])
                    return f
                dv_P[i] = E('v', pacc(i, b),
                            waits=[("sAC", ac_ksc)], inc='sDV')
                if i + 2 < NCH:
                    emit_xload(i + 2)

            reord = [A_row[0:1, 0:1024:2], A_row[0:1, 1:1024:2],
                     A_row[0:1, 1024:1536], A_row[0:1, 1536:2048]]
            dv_ar = [0] * 4
            for n in range(4):
                w = [("sDV", dv_P[NCH - 1])]
                if n >= 1:
                    w.append(("sDV", dv_ar[n - 1]))
                pe_n = E('t', (lambda n_: lambda e: e.matmul(
                    ps_a[:], ones_c[:], acc[:, n_ * 512:(n_ + 1) * 512],
                    start=True, stop=True))(n), waits=w, inc='sPE')
                dv_ar[n] = E('v', (lambda n_: lambda e: e.tensor_copy(
                    out=reord[n_], in_=ps_a[:]))(n), waits=[("sPE", pe_n)], inc='sDV')

            def acolmm(e):
                r = None
                for fo in range(16):
                    r = e.matmul(ps_ac[:, fo:fo + 1], A_row[0:1, fo * P:(fo + 1) * P],
                                 ones_f[0:1, 0:1], start=True, stop=True)
                return r
            pe_acol = E('t', acolmm, waits=[("sDV", dv_ar[3])], inc='sPE')
            ac_A = E('s', lambda e: e.activation(out=A_col[:], in_=ps_ac[:], func=AF.Copy,
                                                 scale=0.5, bias=0.5),
                     waits=[("sPE", pe_acol)], inc='sAC')

            bar1 = barrier()

        with ExitStack() as p2a:
            e2 = p2a.enter_context
            tmp2 = e2(nc.sbuf_tensor("tmp2", [P, D], F32))
            bq_brow = e2(nc.sbuf_tensor("bq_brow", [1, D], BF16))
            bout_brow = e2(nc.sbuf_tensor("bout_brow", [1, D], BF16))
            row_f = tmp2[0:1, :]
            wq_sb = e2(nc.sbuf_tensor("wq_sb", [P, 16 * D], BF16))
            woa_sb = [e2(nc.sbuf_tensor(f"woa{i}", [P, 16 * 512], BF16)) for i in range(2)]
            x2T = [e2(nc.sbuf_tensor(f"x2T{i}", [P, 16 * 512], BF16)) for i in range(2)]
            qbT = e2(nc.sbuf_tensor("qbT", [P, 16 * 512], BF16))
            xn_sb = e2(nc.sbuf_tensor("xn_sb", [P, D], F32))
            xnT_half = e2(nc.sbuf_tensor("xnT_half", [P, 1024], F32))
            s0_buf = [e2(nc.sbuf_tensor(f"s0b{i}", [P, 512], BF16)) for i in range(2)]
            x2b = e2(nc.sbuf_tensor("x2b", [P, D], BF16))
            xnT_bf = x2b
            ps_t2 = e2(nc.psum_tensor("ps_t2", [P, D], BF16))
            ps_tf = e2(nc.psum_tensor("ps_tf", [P, 1024], F32))
            ps_r = e2(nc.psum_tensor("ps_r", [P, 16], F32))
            ps_mm = [e2(nc.psum_tensor(f"ps_mm{i}", [P, 512], F32)) for i in range(2)]

            dv_ln2a = [0] * NCH
            dv_x2ev2 = [0] * NCH
            dv_xn2a = [0] * NCH
            pe_x2T = [0] * NCH
            pe_xnT = [0] * NCH
            ac_probs = [0] * NCH
            ac_rstd2 = [0] * NCH
            st_xn = [0] * NCH
            st_s0_piece = {}
            sx2 = [0] * NCH
            qbank = {0: ("sDV", 0), 1: ("sDV", 0)}

            def emit_xload2(ci):
                b = ci % 2
                wx = [("sDV", dv_ln2a[ci - 2])] if ci >= 2 else [("sBAR", bar1)]
                sx2[ci] = E('y', (lambda c, b_: lambda e: e.dma_start(
                    out=x_sb[b_][:], in_=x_h[c * P:(c + 1) * P, :]))(ci, b),
                    waits=wx, inc='sX', n=16)

            def emit_chain(ci):
                b = ci % 2
                tb = (ci // 4) % 2
                j = ci % 4
                mu = mv[:, 2 * b: 2 * b + 1]
                var = mv[:, 2 * b + 1: 2 * b + 2]
                rstd1 = rst[:, 2 * b: 2 * b + 1]
                rstdc = rst[:, 2 * b + 1: 2 * b + 2]
                E('v', (lambda b_: lambda e: e.bn_stats(
                    out=st6[:, 0:6], in_=x_sb[b_][:, 0:512]))(b),
                    waits=[("sX", sx2[ci]), ("sBAR", bar1)])
                dv_b = 0
                for q in range(1, 4):
                    dv_b = E('v', (lambda b_, q_: lambda e: e.bn_stats(
                        out=st6[:, 6 * q_:6 * (q_ + 1)],
                        in_=x_sb[b_][:, 512 * q_:512 * (q_ + 1)]))(b, q),
                        inc='sDV' if q == 3 else None)
                wagg = [("sDV", dv_b)]
                if ci >= 2:
                    wagg.append(("sAC", ac_rstd2[ci - 2]))
                dv_m = E('v', (lambda b_: lambda e: e.bn_aggr(
                    out=mv[:, 2 * b_: 2 * b_ + 2], in_=st6[:]))(b),
                    waits=wagg, inc='sDV')
                a1 = E('s', (lambda var_: lambda e: e.activation(
                    out=lt[:, 0:1], in_=var_, func=AF.Ln, bias=eps_t[:]))(var),
                    waits=[("sDV", dv_m), ("sBAR", bar1)], inc='sAC')
                a2 = E('s', (lambda var_: lambda e: e.activation(
                    out=lt[:, 1:2], in_=var_, func=AF.Ln, scale=1.0 + EPS,
                    bias=eps2_t[:]))(var),
                    waits=[("sAC", a1)], inc='sAC')
                a3 = E('s', (lambda r1: lambda e: e.activation(
                    out=r1, in_=lt[:, 0:1], func=AF.Exp, scale=-0.5))(rstd1),
                    waits=[("sAC", a2)], inc='sAC')
                ac_rstd2[ci] = E('s', (lambda rc: lambda e: e.activation(
                    out=rc, in_=lt[:, 1:2], func=AF.Exp, scale=-0.5))(rstdc),
                    waits=[("sAC", a3)], inc='sAC')
                E('s', (lambda c, r1: lambda e: e.activation(
                    out=rstd_all[:, c:c + 1], in_=r1, func=AF.Copy, scale=1.0))(ci, rstd1),
                    inc='sAC')
                E('s', (lambda c, mu_: lambda e: e.activation(
                    out=mu_all[:, c:c + 1], in_=mu_, func=AF.Copy, scale=1.0))(ci, mu),
                    inc='sAC')

                wxn = [("sAC", ac_rstd2[ci])]
                if ci >= 1:
                    wxn.append(("sPE", pe_xnT[ci - 1]))
                dv_xn2a[ci] = E('v', (lambda b_, mu_, r1: lambda e: e.tensor_scalar(
                    out=xn_sb[:], in0=x_sb[b_][:], scalar1=mu_, scalar2=r1,
                    op0=ALU.subtract, op1=ALU.mult))(b, mu, rstd1),
                    waits=wxn, inc='sDV')
                dv_ln2a[ci] = E('v', (lambda b_, mu_: lambda e: e.scalar_tensor_tensor(
                    out=tmp2[:], in0=x_sb[b_][:], scalar=mu_, in1=gam_b[:],
                    op0=ALU.subtract, op1=ALU.mult))(b, mu), inc='sDV')
                wx2 = []
                if ci >= 1:
                    wx2.append(("sPE", pe_x2T[ci - 1]))
                    wx2.append(("sST", st_xn[ci - 1]))
                dv_x2 = E('v', (lambda rc: lambda e: e.scalar_tensor_tensor(
                    out=x2b[:], in0=tmp2[:], scalar=rc, in1=bet_b[:],
                    op0=ALU.mult, op1=ALU.add))(rstdc), waits=wx2, inc='sDV')

                def tr2(e):
                    r = None
                    for fo in range(16):
                        r = e.transpose(ps_t2[:, fo * P:(fo + 1) * P],
                                        x2b[:, fo * P:(fo + 1) * P], idb[:])
                    return r
                wt2 = [("sDV", dv_x2)]
                if ci >= 1:
                    wt2.append(("sDV", dv_x2ev2[ci - 1]))
                pe_x2T[ci] = E('t', tr2, waits=wt2, inc='sPE')
                dv_x2ev2[ci] = E('v', (lambda j_, tb_: lambda e: e.tensor_copy(
                    out=x2T[tb_][:].rearrange("p (k n) -> p k n", k=16)[:, :, j_ * P:(j_ + 1) * P],
                    in_=ps_t2[:].rearrange("p (k n) -> p k n", k=16)))(j, tb),
                    waits=[("sPE", pe_x2T[ci])], inc='sDV')

                dv_bf = 0
                pe_rt = 0
                for h in range(2):
                    def trx(h_):
                        def f(e):
                            r = None
                            for fo in range(8):
                                fa = h_ * 8 + fo
                                r = e.transpose(ps_tf[:, fo * P:(fo + 1) * P],
                                                xn_sb[:, fa * P:(fa + 1) * P], idf[:])
                            return r
                        return f
                    wtx = [("sDV", dv_xn2a[ci])]
                    if h == 1:
                        wtx.append(("sDV", dv_bf))
                    pe_xnT[ci] = E('t', trx(h), waits=wtx, inc='sPE')
                    pe_h = pe_xnT[ci]
                    dv_xh = E('v', lambda e: e.tensor_copy(out=xnT_half[:], in_=ps_tf[:]),
                              waits=[("sPE", pe_h)], inc='sDV')
                    wbf = [("sPE", pe_h)]
                    if ci >= 1:
                        wbf.append(("sST", st_xn[ci - 1]))
                    dv_bf = E('v', (lambda h_: lambda e: e.tensor_copy(
                        out=xnT_bf[:, h_ * 1024:(h_ + 1) * 1024], in_=ps_tf[:]))(h),
                        waits=wbf, inc='sDV')

                    def rtmm(h_):
                        def f(e):
                            r = None
                            for kc in range(8):
                                ka = h_ * 8 + kc
                                r = e.matmul(ps_r[:], xnT_half[:, kc * P:(kc + 1) * P],
                                             wrt_sb[:, ka * 16:(ka + 1) * 16],
                                             start=(ka == 0), stop=(ka == 15))
                            return r
                        return f
                    wrtw = [("sDV", dv_xh), ("sW", w_cst)]
                    if ci >= 1:
                        wrtw.append(("sAC", ac_probs[ci - 1]))
                    pe_rt = E('t', rtmm(h), waits=wrtw, inc='sPE')

                st_xn[ci] = E('y', (lambda c: lambda e: e.dma_start(
                    out=xnTh[:, :, c * P:(c + 1) * P].rearrange("k p n -> p k n"),
                    in_=xnT_bf[:].rearrange("p (k n) -> p k n", k=16)))(ci),
                    waits=[("sDV", dv_bf)], inc='sST', n=16)

                d1 = E('v', lambda e: e.reduce_max(out=negm[:], in_=ps_r[:], axis=AX.X,
                                                   negate=True),
                       waits=[("sPE", pe_rt)], inc='sDV')
                ac_probs[ci] = E('s', lambda e: e.activation(out=probs[:], in_=ps_r[:],
                                                             func=AF.Exp, bias=negm[:]),
                                 waits=[("sDV", d1)], inc='sAC')
                d2 = E('v', lambda e: e.reduce_sum(out=ssum[:, 0:1], in_=probs[:], axis=AX.X),
                       waits=[("sAC", ac_probs[ci])], inc='sDV')
                d3 = E('v', lambda e: e.reciprocal(out=ssum[:, 1:2], in_=ssum[:, 0:1]),
                       waits=[("sDV", d2)], inc='sDV')
                d4 = E('v', lambda e: e.tensor_scalar_mul(probs[:], probs[:], ssum[:, 1:2]),
                       waits=[("sDV", d3)], inc='sDV')
                d5 = E('v', lambda e: e.max(out=m8[:], in_=probs[:]),
                       waits=[("sDV", d4)], inc='sDV')
                d6 = E('v', lambda e: e.tensor_scalar(out=msk[:], in0=probs[:],
                                                      scalar1=m8[:, 1:2], scalar2=None,
                                                      op0=ALU.is_ge),
                       waits=[("sDV", d5)], inc='sDV')
                E('v', (lambda c: lambda e: e.tensor_mul(
                    out=gate_all[:, c * 16:(c + 1) * 16], in0=msk[:], in1=probs[:]))(c=ci),
                    waits=[("sDV", d6)], inc='sDV')
                if ci + 2 < NCH:
                    emit_xload2(ci + 2)

            w_r1 = E('y', lambda e: e.dma_start(out=row_f, in_=bq_row_h),
                     waits=[("sBAR", bar1)], inc='sW', n=16)
            dv_r1 = E('v', lambda e: e.tensor_copy(out=bq_brow[:], in_=row_f),
                      waits=[("sW", w_r1), ("sBAR", bar1)], inc='sDV')
            w_r2 = E('y', lambda e: e.dma_start(out=row_f, in_=bout_h),
                     waits=[("sDV", dv_r1)], inc='sW', n=16)
            w_r2d = E('v', lambda e: e.tensor_copy(out=bout_brow[:], in_=row_f),
                      waits=[("sW", w_r2)], inc='sDV')
            pe_woa_used = {0: 0, 1: 0}
            w_wq = E('y', lambda e: e.dma_start(
                out=wq_sb[:].rearrange("p (k n) -> p k n", k=16),
                in_=wqbf.rearrange("k p n -> p k n")),
                waits=[("sG", g_wq)], inc='sW', n=16)
            emit_xload2(0)
            emit_xload2(1)
            for ci in range(4):
                emit_chain(ci)

            for t in range(NT):
                tb = t % 2
                next_chains = [4 * (t + 1) + k for k in range(4)] if t + 1 < NT else []
                for mc in range(16):
                    bk = mc % 2
                    w = [("sDV", dv_x2ev2[4 * t + 3]), ("sW", w_wq), qbank[bk]]

                    def qmm(mc_, tb_):
                        def f(e):
                            for kc in range(16):
                                e.matmul(ps_mm[mc_ % 2][:],
                                         wq_sb[:, kc * D + mc_ * P: kc * D + (mc_ + 1) * P],
                                         x2T[tb_][:, kc * 512:(kc + 1) * 512],
                                         start=(kc == 0), stop=False)
                            for n4 in range(3):
                                e.matmul(ps_mm[mc_ % 2][:, n4 * P:(n4 + 1) * P],
                                         bq_brow[0:1, mc_ * P:(mc_ + 1) * P],
                                         ones_b128[0:1, :], start=False, stop=False)
                            return e.matmul(ps_mm[mc_ % 2][:, 3 * P:4 * P],
                                            bq_brow[0:1, mc_ * P:(mc_ + 1) * P],
                                            ones_b128[0:1, :], start=False, stop=True)
                        return f
                    pe_q = E('t', qmm(mc, tb), waits=w, inc='sPE')
                    ac_q = E('s', (lambda mc_: lambda e: e.activation(
                        out=qbT[:, mc_ * 512:(mc_ + 1) * 512], in_=ps_mm[mc_ % 2][:],
                        func=AF.Copy, scale=A_col[:, mc_:mc_ + 1]))(mc),
                        waits=[("sPE", pe_q), ("sAC", ac_A)], inc='sAC')
                    qbank[bk] = ("sAC", ac_q)
                    if mc == 7 and next_chains:
                        emit_chain(next_chains[0])
                    if mc == 15 and next_chains:
                        emit_chain(next_chains[1])
                ac_q15 = qbank[1]

                w_woa_t = [0] * 4
                for nch in range(2):
                    wb = nch % 2
                    wl = [("sG", g_woa), ("sPE", pe_woa_used.get(wb, 0)), ("sBAR", bar1)]
                    w_woa_t[nch] = E('y', (lambda n_, b_: lambda e: e.dma_start(
                        out=woa_sb[b_][:].rearrange("p (k n) -> p k n", k=16),
                        in_=woabf[:, :, n_ * 512:(n_ + 1) * 512].rearrange("k p n -> p k n")))(nch, wb),
                        waits=wl, inc='sW', n=16)
                for nch in range(4):
                    wb = nch % 2
                    if nch >= 2:
                        wl = [("sG", g_woa), ("sPE", pe_woa_used.get(wb, 0))]
                        w_woa_t[nch] = E('y', (lambda n_, b_: lambda e: e.dma_start(
                            out=woa_sb[b_][:].rearrange("p (k n) -> p k n", k=16),
                            in_=woabf[:, :, n_ * 512:(n_ + 1) * 512].rearrange("k p n -> p k n")))(nch, wb),
                            waits=wl, inc='sW', n=16)
                    w_woa = w_woa_t[nch]
                    for j in range(4):
                        bk = j % 2
                        w = [("sW", w_woa), ac_q15, qbank[bk]]

                        def wmm(n_, j_):
                            def f(e):
                                for kc in range(16):
                                    e.matmul(ps_mm[j_ % 2][:],
                                             qbT[:, kc * 512 + j_ * P: kc * 512 + (j_ + 1) * P],
                                             woa_sb[n_ % 2][:, kc * 512:(kc + 1) * 512],
                                             start=(kc == 0), stop=False)
                                return e.matmul(ps_mm[j_ % 2][:],
                                                ones_b128[0:1, :],
                                                bout_brow[0:1, n_ * 512:(n_ + 1) * 512],
                                                start=False, stop=True)
                            return f
                        pe_wj = E('t', wmm(nch, j), waits=w, inc='sPE')
                        pe_woa_used[wb] = pe_wj
                        sb = (nch * 4 + j) % 2
                        pidx = nch * 4 + j
                        ws0 = [("sPE", pe_wj)]
                        if pidx >= 2:
                            ws0.append(("sST", st_s0_piece.get((t, pidx - 2), 0)))
                        elif t >= 1:
                            ws0.append(("sST", st_s0_piece.get((t - 1, 14 + pidx), 0)))
                        ac_s0 = E('s', (lambda j_, b_: lambda e: e.activation(
                            out=s0_buf[b_][:], in_=ps_mm[j_ % 2][:], func=AF.Copy,
                            scale=1.0))(j, sb),
                            waits=ws0, inc='sAC')
                        qbank[bk] = ("sAC", ac_s0)
                        st_s0_piece[(t, pidx)] = E('y', (lambda n_, c, b_: lambda e: e.dma_start(
                            out=s0h[c][:, n_ * 512:(n_ + 1) * 512], in_=s0_buf[b_][:]))(
                                nch, 4 * t + j, sb),
                            waits=[("sAC", ac_s0)], inc='sST', n=16)
                    if nch < 2 and next_chains:
                        emit_chain(next_chains[2 + nch])

            st_2a_total = now('sST')
            bar2 = barrier()

        with ExitStack() as p2b:
            e3 = p2b.enter_context
            xnT_sb = e3(nc.sbuf_tensor("xnT_sb", [P, 16 * 512], BF16))
            s0_sb = [e3(nc.sbuf_tensor(f"s0s{i}", [P, D], BF16)) for i in range(2)]
            acc2 = e3(nc.sbuf_tensor("acc2", [P, 4 * D], F32))
            wup_sb = [e3(nc.sbuf_tensor(f"wup{i}", [P, 16 * 256], BF16)) for i in range(NUPB)]
            wdn_sb = [e3(nc.sbuf_tensor(f"wdn{i}", [P, 8 * D], BF16)) for i in range(2)]
            mid = [e3(nc.sbuf_tensor(f"mid{i}", [P, 8 * 512], BF16)) for i in range(2)]
            gbc = [e3(nc.sbuf_tensor(f"gbc{i}", [P, 512], BF16)) for i in range(2)]
            gateT = e3(nc.sbuf_tensor("gateT", [16, 512], BF16))
            gtmp = [e3(nc.sbuf_tensor(f"gtmp{i}", [P, 512], BF16)) for i in range(2)]
            g_buf = e3(nc.sbuf_tensor("g_buf", [P, D], F32))
            sel_sb = e3(nc.sbuf_tensor("sel_sb", [16, 16 * P], BF16))
            ps_up = [e3(nc.psum_tensor(f"ps_up{i}", [P, 512], F32)) for i in range(2)]
            ps_dn = [e3(nc.psum_tensor(f"ps_dn{i}", [P, 1024], F32)) for i in range(2)]
            ps_gb = e3(nc.psum_tensor("ps_gb", [P, 512], F32))
            ps_gt = e3(nc.psum_tensor("ps_gt", [16, 512], F32))

            w_sel = E('y', lambda e: e.dma_start(out=sel_sb[:], in_=sel_h),
                      waits=[("sBAR", bar2)], inc='sW', n=16)

            pe_dn_done_g = {}
            pe_up_piece = {}
            ac_gelu_gm = {}
            dv_gbc_g = {}
            dv_scale_gr = {}
            dv_dnev_gf = {}
            ac_silu = {}
            dv_comb = {}
            out_free = {}
            gr_tmp_prev = {0: 0, 1: 0}
            dv_gtev = [0] * NT
            pe_up_last = [0] * NT
            w2_up = {}
            dv3_prev = 0

            def emit_wup_load(t, ee, m2):
                gp = (t * NE + ee) * 4 + m2
                pb = gp % NUPB
                wu = [("sG", g_wup[ee]), ("sPE", pe_up_piece.get(gp - NUPB, 0))]
                w2_up[gp] = E('s', (lambda e_, m_, b_: lambda e: e.dma_start(
                    out=wup_sb[b_][:].rearrange("p (k n) -> p k n", k=16),
                    in_=wupbf[e_][:, :, m_ * 256:(m_ + 1) * 256].rearrange("k p n -> p k n")))(ee, m2, pb),
                    waits=wu, inc='sW2', n=16)

            for t in range(NT):
                g0 = t * NG
                wxt = [("sST", st_xn[4 * t + 3])]
                if t >= 1:
                    wxt.append(("sPE", pe_up_last[t - 1]))
                else:
                    wxt.append(("sBAR", bar2))
                w_xnT = E('y', (lambda t_: lambda e: e.dma_start(
                    out=xnT_sb[:].rearrange("p (k n) -> p k n", k=16),
                    in_=xnTh[:, :, t_ * 512:(t_ + 1) * 512].rearrange("k p n -> p k n")))(t),
                    waits=wxt, inc='sW', n=16)

                wgt = [("sBAR", bar2)] if t == 0 else [("sDV", dv_gtev[t - 1])]

                def gtf(t_):
                    def f(e):
                        r = None
                        for j in range(4):
                            c = 4 * t_ + j
                            r = e.transpose(ps_gt[:, j * P:(j + 1) * P],
                                            gate_all[:, c * 16:(c + 1) * 16], idf[:])
                        return r
                    return f
                pe_gt = E('t', gtf(t), waits=wgt, inc='sPE')
                dv_gtev[t] = E('v', lambda e: e.tensor_copy(out=gateT[:], in_=ps_gt[:]),
                               waits=[("sPE", pe_gt)], inc='sDV')

                if t == 0:
                    emit_wup_load(0, 0, 0)
                    emit_wup_load(0, 0, 1)

                for ee in range(NE):
                    g = ee // 2
                    ei = ee % 2
                    gg = g0 + g
                    ge = t * NE + ee

                    wd = [("sG", g_wdn[ee]), ("sPE", pe_dn_done_g.get(gg - 1, 0))]
                    w_dn_ld = E('y', (lambda e_, b_: lambda e: e.dma_start(
                        out=wdn_sb[b_][:].rearrange("p (k n) -> p k n", k=8),
                        in_=wdnbf[e_].rearrange("k p n -> p k n")))(ee, ee % 2),
                        waits=wd, inc='sW', n=16)

                    if ee >= 2:
                        gr = t * 16 + (ee - 2)
                        wgb = [("sDV", dv_gtev[t]), ("sW", w_sel),
                               ("sDV", dv_scale_gr.get(gr - 2, 0))]
                        pe_g = E('t', (lambda e_: lambda e: e.matmul(
                            ps_gb[:], sel_sb[:, (e_ - 2) * P:(e_ - 1) * P], gateT[:],
                            start=True, stop=True))(ee),
                            waits=wgb, inc='sPE')
                        dv_gbc_g[gr] = E('v', (lambda b_: lambda e: e.tensor_copy(
                            out=gbc[b_][:], in_=ps_gb[:]))(ee % 2),
                            waits=[("sPE", pe_g)], inc='sDV')

                    for m2 in range(4):
                        gp = ge * 4 + m2
                        pb = gp % NUPB
                        nt_, ne_, nm_ = t, ee, m2 + 2
                        if nm_ >= 4:
                            nm_ -= 4
                            ne_ += 1
                            if ne_ >= NE:
                                ne_ = 0
                                nt_ += 1
                        if nt_ < NT:
                            emit_wup_load(nt_, ne_, nm_)
                        for mc in range(2):
                            ma = m2 * 2 + mc
                            gm = ge * 8 + ma
                            ub = gm % 2
                            wmm_ = [("sW2", w2_up[gp]), ("sW", w_xnT),
                                    ("sAC", ac_gelu_gm.get(gm - 2, 0))]

                            def upmm(mc_, b_, ub_):
                                def f(e):
                                    r = None
                                    for kc in range(16):
                                        r = e.matmul(
                                            ps_up[ub_][:],
                                            wup_sb[b_][:, kc * 256 + mc_ * P: kc * 256 + (mc_ + 1) * P],
                                            xnT_sb[:, kc * 512:(kc + 1) * 512],
                                            start=(kc == 0), stop=(kc == 15))
                                    return r
                                return f
                            pe_this = E('t', upmm(mc, pb, ub), waits=wmm_, inc='sPE')
                            pe_up_piece[gp] = pe_this
                            if ee == NE - 1 and m2 == 3:
                                pe_up_last[t] = pe_this

                            wge = [("sPE", pe_this), ("sPE", pe_dn_done_g.get(gg - 1, 0))]
                            if ee < 2:
                                ac_gelu_gm[gm] = E('s', (lambda ma_, ei_, ub_: lambda e: e.activation(
                                    out=mid[ei_][:, ma_ * 512:(ma_ + 1) * 512],
                                    in_=ps_up[ub_][:], func=AF.Gelu))(ma, ei, ub),
                                    waits=wge, inc='sAC')
                            else:
                                gr = t * 16 + (ee - 2)
                                grt = gr * 8 + ma
                                wge.append(("sDV", gr_tmp_prev[grt % 2]))
                                ac_gelu_gm[gm] = E('s', (lambda ma_, ub_: lambda e: e.activation(
                                    out=gtmp[ma_ % 2][:], in_=ps_up[ub_][:], func=AF.Gelu))(ma, ub),
                                    waits=wge, inc='sAC')
                                wsc = [("sAC", ac_gelu_gm[gm]), ("sDV", dv_gbc_g[gr]),
                                       ("sPE", pe_dn_done_g.get(gg - 1, 0))]
                                dv_sc = E('v', (lambda ma_, ei_, b_: lambda e: e.tensor_mul(
                                    out=mid[ei_][:, ma_ * 512:(ma_ + 1) * 512],
                                    in0=gtmp[ma_ % 2][:], in1=gbc[b_][:]))(ma, ei, ee % 2),
                                    waits=wsc, inc='sDV')
                                gr_tmp_prev[grt % 2] = dv_sc
                                dv_scale_gr[gr] = dv_sc

                    if ei == 1:
                        mid_ready = []
                        if ee == 1:
                            mid_ready.append(("sAC", ac_gelu_gm[(t * NE + 1) * 8 + 7]))
                        else:
                            mid_ready.append(("sDV", dv_scale_gr[t * 16 + (ee - 2)]))
                        for j in range(4):
                            for nh in range(2):
                                q = 2 * j + nh
                                gf = gg * 8 + q
                                r = q % 2
                                wdn_w = [("sW", w_dn_ld),
                                         ("sDV", dv_dnev_gf.get(gf - 2, 0))] + mid_ready

                                def dnmm(g_, j_, nh_, r_):
                                    def f(e):
                                        rr = None
                                        for ei2 in range(2):
                                            eb = (g_ * 2 + ei2) % 2
                                            for kc in range(8):
                                                for n2 in range(2):
                                                    rr = e.matmul(
                                                        ps_dn[r_][:, n2 * 512:(n2 + 1) * 512],
                                                        mid[ei2][:, kc * 512 + j_ * P: kc * 512 + (j_ + 1) * P],
                                                        wdn_sb[eb][:, kc * D + nh_ * 1024 + n2 * 512: kc * D + nh_ * 1024 + (n2 + 1) * 512],
                                                        start=(ei2 == 0 and kc == 0),
                                                        stop=(ei2 == 1 and kc == 7),
                                                        skip_group_check=True)
                                        return rr
                                    return f
                                pe_f = E('t', dnmm(g, j, nh, r), waits=wdn_w, inc='sPE')
                                if q == 7:
                                    pe_dn_done_g[gg] = pe_f
                                wev = [("sPE", pe_f)]
                                if g == 0:
                                    wev.append(("sAC", ac_silu.get((t - 1, j), 0)))
                                    wev.append(("sDV", dv_comb.get((t - 1, j), 0)))

                                def dnev(g_, j_, nh_, r_):
                                    def f(e):
                                        dst = acc2[:, j_ * D + nh_ * 1024: j_ * D + (nh_ + 1) * 1024]
                                        if g_ == 0:
                                            return e.tensor_copy(out=dst, in_=ps_dn[r_][:])
                                        return e.tensor_add(out=dst, in0=dst, in1=ps_dn[r_][:])
                                    return f
                                dv_dnev_gf[gf] = E('v', dnev(g, j, nh, r),
                                                   waits=wev, inc='sDV')

                g_x = [0] * 4
                g_s0 = [0] * 4

                def emit_comb_loads(j):
                    ci = 4 * t + j
                    b = j % 2
                    if t == 0 and j < 2:
                        wxl = [("sBAR", bar2)]
                    elif j < 2:
                        wxl = [("sOUT", out_free.get((t - 1, j + 2), 0))]
                    else:
                        wxl = [("sOUT", out_free.get((t, j - 2), 0))]
                    g_x[j] = E('g', (lambda c, b_: lambda e: e.dma_start(
                        out=x_sb[b_][:], in_=x_h[c * P:(c + 1) * P, :]))(ci, b),
                        waits=wxl, inc='sG', n=16)
                    ws0l = [("sST", st_2a_total)]
                    prevc = dv_comb.get((t - 1, j + 2) if j < 2 else (t, j - 2), 0)
                    if prevc:
                        ws0l.append(("sDV", prevc))
                    g_s0[j] = E('g', (lambda c, b_: lambda e: e.dma_start(
                        out=s0_sb[b_][:], in_=s0h[c]))(ci, b),
                        waits=ws0l, inc='sG', n=16)

                emit_comb_loads(0)
                emit_comb_loads(1)
                for j in range(4):
                    ci = 4 * t + j
                    b = j % 2
                    wsl = [("sDV", dv_dnev_gf[(g0 + NG - 1) * 8 + 2 * j + 1]),
                           ("sDV", dv3_prev),
                           ("sOUT", out_free.get((t - 1, j + 2) if j < 2 else (t, j - 2), 0))]
                    if j >= 1:
                        wsl.append(("sDV", dv_comb.get((t, j - 1), 0)))
                    ac_silu[(t, j)] = E('s', (lambda j_: lambda e: e.activation(
                        out=g_buf[:], in_=acc2[:, j_ * D:(j_ + 1) * D], func=AF.Silu))(j),
                        waits=wsl, inc='sAC')
                    dv1 = E('v', (lambda j_, b_, c: lambda e: e.tensor_scalar(
                        out=acc2[:, j_ * D:(j_ + 1) * D], in0=x_sb[b_][:],
                        scalar1=mu_all[:, c:c + 1], scalar2=rstd_all[:, c:c + 1],
                        op0=ALU.subtract, op1=ALU.mult))(j, b, ci),
                        waits=[("sG", g_x[j]), ("sAC", ac_silu[(t, j)])], inc='sDV')
                    dv2 = E('v', (lambda j_, b_: lambda e: e.tensor_add(
                        out=acc2[:, j_ * D:(j_ + 1) * D], in0=acc2[:, j_ * D:(j_ + 1) * D],
                        in1=s0_sb[b_][:]))(j, b),
                        waits=[("sG", g_s0[j])], inc='sDV')
                    dv3 = E('v', (lambda j_: lambda e: e.tensor_mul(
                        out=g_buf[:], in0=acc2[:, j_ * D:(j_ + 1) * D], in1=g_buf[:]))(j),
                        waits=[("sDV", dv2), ("sAC", ac_silu[(t, j)])], inc='sDV')
                    dv3_prev = dv3
                    dv_comb[(t, j)] = E('v', (lambda b_: lambda e: e.tensor_add(
                        out=x_sb[b_][:], in0=g_buf[:], in1=x_sb[b_][:]))(b),
                        waits=[("sDV", dv3)], inc='sDV')
                    out_free[(t, j)] = E('g', (lambda c, b_: lambda e: e.dma_start(
                        out=y_h[c * P:(c + 1) * P, :], in_=x_sb[b_][:]))(ci, b),
                        waits=[("sDV", dv_comb[(t, j)])], inc='sOUT', n=16)
                    if j + 2 < 4:
                        emit_comb_loads(j + 2)

            E('g', lambda e: e.memset(bar_t[0:1, 3:4], 0.0),
                waits=[("sOUT", now('sOUT'))])

        with nc.Block() as block:
            @block.tensor
            def _(eng):
                sch.replay('t', eng, sems)

            @block.vector
            def _(eng):
                sch.replay('v', eng, sems)

            @block.scalar
            def _(eng):
                sch.replay('s', eng, sems)

            @block.gpsimd
            def _(eng):
                sch.replay('g', eng, sems)

            @block.sync
            def _(eng):
                sch.replay('y', eng, sems)

    return nc


def _host_inputs(inputs):
    W_in = np.ascontiguousarray(np.asarray(inputs["L0_attn_W_in"], dtype=np.float32))
    b_in = np.asarray(inputs["L0_attn_b_in"], dtype=np.float32)
    perm = np.concatenate([np.arange(0, 1024, 2), np.arange(1, 1024, 2),
                           np.arange(1024, 2048)])
    wq = np.ascontiguousarray(W_in[:, 0:2048])
    wkv = np.ascontiguousarray(
        np.concatenate([W_in[:, 2048:4096][:, perm], W_in[:, 4096:6144][:, perm]], axis=1))
    bq_row = np.ascontiguousarray(b_in[0:2048][None, :])
    bkv_row = np.ascontiguousarray(
        np.concatenate([b_in[2048:4096][perm], b_in[4096:6144][perm]])[None, :])
    gam_row = np.ascontiguousarray(np.asarray(inputs["L0_attn_gamma"], np.float32)[None, :])
    bet_row = np.ascontiguousarray(np.asarray(inputs["L0_attn_beta"], np.float32)[None, :])
    bout_row = np.ascontiguousarray(np.asarray(inputs["L0_attn_b_out"], np.float32)[None, :])
    wout = np.ascontiguousarray(np.asarray(inputs["L0_attn_W_out"], np.float32))
    wrt = np.ascontiguousarray(np.asarray(inputs["L0_ffn_W_router"], np.float32))
    wup = np.ascontiguousarray(np.concatenate(
        [np.asarray(inputs["L0_ffn_Wsh_up"], np.float32),
         np.asarray(inputs["L0_ffn_Wrt_up"], np.float32)], axis=0))
    wdn = np.ascontiguousarray(np.concatenate(
        [np.asarray(inputs["L0_ffn_Wsh_down"], np.float32),
         np.asarray(inputs["L0_ffn_Wrt_down"], np.float32)], axis=0))

    inv = 1.0 / (10000.0 ** (np.arange(0, 1024, 2, dtype=np.float64) / 1024.0))
    fr = np.arange(S, dtype=np.float64)[:, None] * inv[None, :]
    cs = np.concatenate([np.cos(fr), np.sin(fr)], axis=1).astype(BF)
    idf = np.eye(P, dtype=np.float32)
    idb = np.eye(P, dtype=BF)
    sel = np.zeros((16, 16 * P), dtype=BF)
    for ex in range(16):
        sel[ex, ex * P:(ex + 1) * P] = 1.0

    return dict(wq=wq, wkv=wkv, bq_row=bq_row, bkv_row=bkv_row, gam_row=gam_row,
                bet_row=bet_row, bout_row=bout_row, wout=wout, wrt=wrt,
                wup=wup, wdn=wdn, cs=np.ascontiguousarray(cs), idf=idf, idb=idb,
                sel=sel)


def kernel(**inputs):
    X = np.asarray(inputs["L0_attn_X"], dtype=np.float32)
    shared = _host_inputs(inputs)
    nc = build_nc()
    in_maps = [dict(x=np.ascontiguousarray(X[c]), **shared) for c in range(8)]
    res = run_bass_kernel_spmd(nc, in_maps, core_ids=list(range(8)))
    out = np.stack([np.asarray(r["y"]) for r in res.results], axis=0)
    return out.astype(np.float32)


# version 23
# speedup vs baseline: 1.0953x; 1.0953x over previous
import numpy as np
import ml_dtypes
from contextlib import ExitStack

import concourse.bass as bass
import concourse.mybir as mybir
from concourse.bass_utils import run_bass_kernel_spmd

F32 = mybir.dt.float32
BF16 = mybir.dt.bfloat16
AF = mybir.ActivationFunctionType
ALU = mybir.AluOpType
AX = mybir.AxisListType

P = 128
S = 4096
D = 2048
NCH = S // P
NT = 8
NE = 18
NG = NE // 2
EPS = 1e-5
BF = ml_dtypes.bfloat16
NUPB = 2


class Sched:
    def __init__(self):
        self.cnt = {}
        self.ops = {k: [] for k in "tvsgy"}

    def emit(self, eng, fn, waits=(), inc=None, n=1):
        w = tuple((s, v) for (s, v) in waits if v and v > 0)
        if inc is not None:
            self.cnt[inc] = self.cnt.get(inc, 0) + n
        self.ops[eng].append((w, fn, inc, n))
        return self.cnt.get(inc, 0)

    def now(self, name):
        return self.cnt.get(name, 0)

    def replay(self, key, eng, sems):
        for w, fn, inc, n in self.ops[key]:
            for s_name, v in w:
                eng.wait_ge(sems[s_name], v)
            inst = fn(eng)
            if inc is not None:
                inst.then_inc(sems[inc], n)


def build_nc():
    nc = bass.Bass("TRN2", target_bir_lowering=False, debug=False)

    x_h = nc.dram_tensor("x", [S, D], F32, kind="ExternalInput").ap()
    wq_h = nc.dram_tensor("wq", [D, D], F32, kind="ExternalInput").ap()
    wkv_h = nc.dram_tensor("wkv", [D, 2 * D], F32, kind="ExternalInput").ap()
    bq_row_h = nc.dram_tensor("bq_row", [1, D], F32, kind="ExternalInput").ap()
    bkv_row_h = nc.dram_tensor("bkv_row", [1, 2 * D], F32, kind="ExternalInput").ap()
    gam_h = nc.dram_tensor("gam_row", [1, D], F32, kind="ExternalInput").ap()
    bet_h = nc.dram_tensor("bet_row", [1, D], F32, kind="ExternalInput").ap()
    bout_h = nc.dram_tensor("bout_row", [1, D], F32, kind="ExternalInput").ap()
    wout_h = nc.dram_tensor("wout", [D, D], F32, kind="ExternalInput").ap()
    wrt_h = nc.dram_tensor("wrt", [D, 16], F32, kind="ExternalInput").ap()
    wup_h = nc.dram_tensor("wup", [NE, D, 1024], F32, kind="ExternalInput").ap()
    wdn_h = nc.dram_tensor("wdn", [NE, 1024, D], F32, kind="ExternalInput").ap()
    cs_h = nc.dram_tensor("cs", [S, 1024], BF16, kind="ExternalInput").ap()
    idf_h = nc.dram_tensor("idf", [P, P], F32, kind="ExternalInput").ap()
    idb_h = nc.dram_tensor("idb", [P, P], BF16, kind="ExternalInput").ap()
    y_h = nc.dram_tensor("y", [S, D], F32, kind="ExternalOutput").ap()

    wupbf = nc.dram_tensor("wupbf", [NE, 16, P, 1024], BF16, kind="Internal").ap()
    wdnbf = nc.dram_tensor("wdnbf", [NE, 8, P, D], BF16, kind="Internal").ap()
    wqbf = nc.dram_tensor("wqbf", [16, P, D], BF16, kind="Internal").ap()
    woabf = nc.dram_tensor("woabf", [16, P, D], BF16, kind="Internal").ap()
    s0h = nc.dram_tensor("s0h", [NCH, P, D], BF16, kind="Internal").ap()
    xnTh = nc.dram_tensor("xnTh", [16, P, S], BF16, kind="Internal").ap()

    sch = Sched()
    E = sch.emit
    now = sch.now

    with ExitStack() as top:
        en = top.enter_context
        x_sb = [en(nc.sbuf_tensor(f"x{i}", [P, D], F32)) for i in range(2)]
        gate_all = en(nc.sbuf_tensor("gate_all", [P, NCH * 16], F32))
        A_col = en(nc.sbuf_tensor("A_col", [P, 16], F32))
        gam_b = en(nc.sbuf_tensor("gam_b", [P, D], BF16))
        bet_b = en(nc.sbuf_tensor("bet_b", [P, D], BF16))
        idf = en(nc.sbuf_tensor("idf_sb", [P, P], F32))
        idb = en(nc.sbuf_tensor("idb_sb", [P, P], BF16))
        ones_f = en(nc.sbuf_tensor("ones_f", [1, P], F32))
        ones_c = en(nc.sbuf_tensor("ones_c", [P, 1], F32))
        eps_t = en(nc.sbuf_tensor("eps_t", [P, 1], F32))
        eps2_t = en(nc.sbuf_tensor("eps2_t", [P, 1], F32))
        st6 = en(nc.sbuf_tensor("st6", [P, 24], F32))
        mv = en(nc.sbuf_tensor("mv", [P, 4], F32))
        rst = en(nc.sbuf_tensor("rst", [P, 4], F32))
        lt = en(nc.sbuf_tensor("lt", [P, 2], F32))
        sq = en(nc.sbuf_tensor("sq", [P, 2], F32))
        rkv = en(nc.sbuf_tensor("rkv", [P, 1], F32))
        ones_b128 = en(nc.sbuf_tensor("ones_b128", [1, P], BF16))
        probs = en(nc.sbuf_tensor("probs", [P, 16], F32))
        negm = en(nc.sbuf_tensor("negm", [P, 1], F32))
        ssum = en(nc.sbuf_tensor("ssum", [P, 2], F32))
        m8 = en(nc.sbuf_tensor("m8", [P, 8], F32))
        msk = en(nc.sbuf_tensor("msk", [P, 16], F32))
        bar_t = en(nc.sbuf_tensor("bar_t", [1, 8], F32))
        mu_all = en(nc.sbuf_tensor("mu_all", [P, NCH], F32))
        rstd_all = en(nc.sbuf_tensor("rstd_all", [P, NCH], F32))

        sem_names = ["sX", "sW", "sW2", "sG", "sPE", "sDV", "sAC", "sPL",
                     "sST", "sOUT", "sBAR"]
        sems = {n_: en(nc.semaphore(n_)) for n_ in sem_names}

        def barrier():
            E('t', lambda e: e.nop(), inc='sBAR')
            E('v', lambda e: e.memset(bar_t[0:1, 0:1], 0.0), inc='sBAR')
            E('s', lambda e: e.activation(out=bar_t[0:1, 1:2], in_=bar_t[0:1, 1:2],
                                          func=AF.Copy, scale=1.0), inc='sBAR')
            return now('sBAR')

        w_cst = 0
        for fn in [
            lambda e: e.dma_start(out=idf[:], in_=idf_h),
            lambda e: e.dma_start(out=idb[:], in_=idb_h),
        ]:
            w_cst = E('y', fn, inc='sW', n=16)

        def memsets(v):
            v.memset(ones_f[:], 1.0)
            v.memset(ones_c[:], 1.0)
            v.memset(ones_b128[:], 1.0)
            v.memset(eps_t[:], EPS)
            return v.memset(eps2_t[:], EPS * EPS)
        dv_ones = E('v', memsets, inc='sDV')

        with ExitStack() as ph01:
            e1 = ph01.enter_context
            bkv_b = e1(nc.sbuf_tensor("bkv_b", [P, 2 * D], BF16))
            pe_bc_last = 0
            with ExitStack() as rows:
                er = rows.enter_context
                gam_row = er(nc.sbuf_tensor("gam_row_sb", [1, D], F32))
                bet_row = er(nc.sbuf_tensor("bet_row_sb", [1, D], F32))
                bkv_row = er(nc.sbuf_tensor("bkv_row_sb", [1, 2 * D], F32))
                bout_row = er(nc.sbuf_tensor("bout_row_sb", [1, D], F32))
                ps0 = [er(nc.psum_tensor(f"ps0_{i}", [P, 512], F32)) for i in range(2)]
                w_rows = 0
                for dst, src in [(gam_row, gam_h), (bet_row, bet_h),
                                 (bkv_row, bkv_row_h), (bout_row, bout_h)]:
                    w_rows = E('y', (lambda d, s_: lambda e: e.dma_start(
                        out=d[:], in_=s_))(dst, src), inc='sW', n=16)
                bc_jobs = (
                    [(gam_b[:, n * 512:(n + 1) * 512], gam_row[0:1, n * 512:(n + 1) * 512]) for n in range(4)]
                    + [(bet_b[:, n * 512:(n + 1) * 512], bet_row[0:1, n * 512:(n + 1) * 512]) for n in range(4)]
                    + [(bkv_b[:, n * 512:(n + 1) * 512],
                        bkv_row[0:1, n * 512:(n + 1) * 512]) for n in range(8)]
                )
                dv_bc = [0] * len(bc_jobs)
                for j, (dst, src) in enumerate(bc_jobs):
                    w = [("sW", w_rows), ("sDV", dv_ones)]
                    if j >= 2:
                        w.append(("sDV", dv_bc[j - 2]))
                    pe_j = E('t', (lambda s_, b: lambda e: e.matmul(
                        ps0[b][:], ones_f[:], s_, start=True, stop=True))(src, j % 2),
                        waits=w, inc='sPE')
                    dv_bc[j] = E('v', (lambda d, b: lambda e: e.tensor_copy(
                        out=d, in_=ps0[b][:]))(dst, j % 2),
                        waits=[("sPE", pe_j)], inc='sDV')
                    pe_bc_last = pe_j

            wkv_sb = e1(nc.sbuf_tensor("wkv_sb", [P, 16 * 2 * D], BF16))
            cs_sb = [e1(nc.sbuf_tensor(f"cs{i}", [P, 1024], BF16)) for i in range(2)]
            xn2b = e1(nc.sbuf_tensor("xn2b", [P, D], BF16))
            xn2T = e1(nc.sbuf_tensor("xn2T", [P, D], BF16))
            K_sb = e1(nc.sbuf_tensor("K_sb", [P, D], BF16))
            V_sb = e1(nc.sbuf_tensor("V_sb", [P, D], BF16))
            acc = e1(nc.sbuf_tensor("acc_ph1", [P, D], F32))
            dump = e1(nc.sbuf_tensor("dump", [P, D], F32))
            sqdump = e1(nc.sbuf_tensor("sqdump", [P, D], BF16))
            kr1, kr2 = dump[:, 0:512], dump[:, 512:1024]
            A_row = dump

            ps_t = e1(nc.psum_tensor("ps_t", [P, D], BF16))
            ps_kv = [e1(nc.psum_tensor(f"ps_kv{i}", [P, 512], F32)) for i in range(2)]
            ps_a = e1(nc.psum_tensor("ps_a", [1, 512], F32))
            ps_ac = e1(nc.psum_tensor("ps_ac", [P, 16], F32))

            g_wkv = E('g', lambda e: e.dma_start(
                out=wkv_sb[:].rearrange("p (k n) -> p k n", k=16),
                in_=wkv_h.rearrange("(k p) n -> p k n", p=P)),
                waits=[("sPE", pe_bc_last)], inc='sG', n=16)
            conv_jobs = []
            wq_r = wq_h.rearrange("(k p) n -> k p n", p=P)
            woa_r = wout_h.rearrange("(k p) n -> k p n", p=P)
            for q in range(4):
                conv_jobs.append((wqbf[4 * q:4 * (q + 1)], wq_r[4 * q:4 * (q + 1)]))
            for q in range(4):
                conv_jobs.append((woabf[4 * q:4 * (q + 1)], woa_r[4 * q:4 * (q + 1)]))
            up_last, dn_last = {}, {}
            for ee in range(NE):
                ur = wup_h[ee].rearrange("(k p) m -> k p m", p=P)
                dr = wdn_h[ee].rearrange("(k p) m -> k p m", p=P)
                for q in range(4):
                    conv_jobs.append((wupbf[ee][4 * q:4 * (q + 1)], ur[4 * q:4 * (q + 1)]))
                    up_last[ee] = len(conv_jobs) - 1
                for q in range(2):
                    conv_jobs.append((wdnbf[ee][4 * q:4 * (q + 1)], dr[4 * q:4 * (q + 1)]))
                    dn_last[ee] = len(conv_jobs) - 1
            njobs = len(conv_jobs)
            g_conv = [0] * njobs
            for jix, (dst, src) in enumerate(conv_jobs):
                thr = 16 * int(4 + jix * (88 - 4) / max(1, njobs - 1)) * 1
                g_conv[jix] = E('g', (lambda d, s_: lambda e: e.dma_start(out=d, in_=s_))(dst, src),
                                waits=[("sX", 16 * min(88, 4 + (jix * 84) // max(1, njobs - 1)))],
                                inc='sG', n=16)
            g_wq = g_conv[3]
            g_woa = g_conv[7]
            g_wup = [g_conv[up_last[ee]] for ee in range(NE)]
            g_wdn = [g_conv[dn_last[ee]] for ee in range(NE)]

            dv_xc = [0] * NCH
            dv_xn2 = [0] * NCH
            dv_krot = [0] * NCH
            pl_vrot = [0] * NCH
            dv_x2ev = [0] * NCH
            pe_tr = [0] * NCH
            dv_P = [0] * NCH
            dv_ev = [[0] * 8 for _ in range(NCH)]
            ac_rkv = [0] * NCH
            ac_rstd = [0] * NCH
            ac_sqv = [0] * NCH
            sx_x = [0] * NCH
            sx_cs = [0] * NCH

            def emit_xload(i):
                b = i % 2
                wx = [("sDV", dv_xc[i - 2]), ("sDV", pl_vrot[i - 2])] if i >= 2 \
                    else [("sPE", pe_bc_last)]
                sx_x[i] = E('y', (lambda i_, b_: lambda e: e.dma_start(
                    out=x_sb[b_][:], in_=x_h[i_ * P:(i_ + 1) * P, :]))(i, b),
                    waits=wx, inc='sX', n=16)
                wc = ([("sDV", pl_vrot[i - 2])] if i >= 2
                      else [("sPE", pe_bc_last)])
                sx_cs[i] = E('y', (lambda i_, b_: lambda e: e.dma_start(
                    out=cs_sb[b_][:], in_=cs_h[i_ * P:(i_ + 1) * P, :]))(i, b),
                    waits=wc, inc='sX', n=16)

            def emit_LN_stats(i):
                b = i % 2
                var = mv[:, 2 * b + 1: 2 * b + 2]
                rstdc = rst[:, 2 * b + 1: 2 * b + 2]
                E('v', (lambda b_: lambda e: e.bn_stats(
                    out=st6[:, 0:6], in_=x_sb[b_][:, 0:512]))(b),
                    waits=[("sX", sx_x[i])])
                dv_bns = 0
                for q in range(1, 4):
                    dv_bns = E('v', (lambda b_, q_: lambda e: e.bn_stats(
                        out=st6[:, 6 * q_:6 * (q_ + 1)],
                        in_=x_sb[b_][:, 512 * q_:512 * (q_ + 1)]))(b, q),
                        inc='sDV' if q == 3 else None)
                wagg = [("sDV", dv_bns)]
                if i >= 2:
                    wagg.append(("sAC", ac_rstd[i - 2]))
                dv_mv = E('v', (lambda b_: lambda e: e.bn_aggr(
                    out=mv[:, 2 * b_: 2 * b_ + 2], in_=st6[:]))(b),
                    waits=wagg, inc='sDV')
                ac_ln = E('s', (lambda var_: lambda e: e.activation(
                    out=lt[:, 1:2], in_=var_, func=AF.Ln,
                    scale=1.0 + EPS, bias=eps2_t[:]))(var),
                    waits=[("sDV", dv_mv)], inc='sAC')
                ac_rstd[i] = E('s', (lambda rc: lambda e: e.activation(
                    out=rc, in_=lt[:, 1:2], func=AF.Exp, scale=-0.5))(rstdc),
                    waits=[("sAC", ac_ln)], inc='sAC')

            def emit_LN_stt(i):
                b = i % 2
                mu = mv[:, 2 * b: 2 * b + 1]
                rstdc = rst[:, 2 * b + 1: 2 * b + 2]
                dv_xc[i] = E('v', (lambda b_, mu_: lambda e: e.scalar_tensor_tensor(
                    out=dump[:], in0=x_sb[b_][:], scalar=mu_, in1=gam_b[:],
                    op0=ALU.subtract, op1=ALU.mult))(b, mu),
                    waits=[("sAC", ac_rstd[i])], inc='sDV')
                wx2 = []
                if i >= 1:
                    wx2.append(("sPE", pe_tr[i - 1]))
                dv_xn2[i] = E('v', (lambda rc: lambda e: e.scalar_tensor_tensor(
                    out=xn2b[:], in0=dump[:], scalar=rc, in1=bet_b[:],
                    op0=ALU.mult, op1=ALU.add))(rstdc), waits=wx2, inc='sDV')

            pend_P = [None]
            emit_xload(0)
            emit_xload(1)
            emit_LN_stats(0)
            emit_LN_stt(0)

            def emit_trans(i):
                def trf(e):
                    r = None
                    for fo in range(16):
                        r = e.transpose(ps_t[:, fo * P:(fo + 1) * P],
                                        xn2b[:, fo * P:(fo + 1) * P], idb[:])
                    return r
                wtr = [("sDV", dv_xn2[i])]
                if i >= 1:
                    wtr.append(("sDV", dv_x2ev[i - 1]))
                pe_tr[i] = E('t', trf, waits=wtr, inc='sPE')
                dv_x2ev[i] = E('v', lambda e: e.tensor_copy(out=xn2T[:], in_=ps_t[:]),
                               waits=[("sPE", pe_tr[i])], inc='sDV')

            emit_trans(0)
            for i in range(NCH):
                b = i % 2
                if i + 1 < NCH:
                    emit_LN_stats(i + 1)

                ev_dst = [K_sb[:, 0:512], K_sb[:, 512:1024], K_sb[:, 1024:1536], K_sb[:, 1536:2048],
                          V_sb[:, 0:512], V_sb[:, 512:1024], V_sb[:, 1024:1536], V_sb[:, 1536:2048]]

                def kvmm(sl_):
                    def f(e):
                        r = None
                        for kc in range(16):
                            r = e.matmul(ps_kv[sl_ % 2][:],
                                         xn2T[:, kc * P:(kc + 1) * P],
                                         wkv_sb[:, kc * (2 * D) + sl_ * 512: kc * (2 * D) + (sl_ + 1) * 512],
                                         start=(kc == 0), stop=(kc == 15))
                        return r
                    return f

                def emit_kv(sl):
                    w = [("sDV", dv_x2ev[i])]
                    if i == 0 and sl == 0:
                        w.append(("sG", g_wkv))
                    if sl >= 2:
                        w.append(("sDV", dv_ev[i][sl - 2]))
                    elif i >= 1:
                        w.append(("sDV", dv_ev[i - 1][sl + 6]))
                    pe_sl = E('t', kvmm(sl), waits=w, inc='sPE')
                    wev_ = [("sPE", pe_sl)]
                    if i >= 1 and sl == 0:
                        wev_.append(("sDV", dv_P[i - 1]))
                    dv_ev[i][sl] = E('v', (lambda sl_, d: lambda e: e.tensor_add(
                        out=d, in0=ps_kv[sl_ % 2][:], in1=bkv_b[:, sl_ * 512:(sl_ + 1) * 512]))(sl, ev_dst[sl]),
                        waits=wev_, inc='sDV')

                emit_kv(0)
                emit_kv(1)

                def krot(b_):
                    def f(e):
                        cos, sin = cs_sb[b_][:, 0:512], cs_sb[b_][:, 512:1024]
                        ke, ko = K_sb[:, 0:512], K_sb[:, 512:1024]
                        e.tensor_mul(out=kr1, in0=ke, in1=cos)
                        e.tensor_mul(out=kr2, in0=ke, in1=sin)
                        e.tensor_mul(out=ke, in0=ko, in1=sin)
                        e.tensor_mul(out=ko, in0=ko, in1=cos)
                        e.tensor_sub(out=ke, in0=kr1, in1=ke)
                        return e.tensor_add(out=ko, in0=ko, in1=kr2)
                    return f
                dv_krot[i] = E('v', krot(b), waits=[("sX", sx_cs[i])], inc='sDV')
                if i + 1 < NCH:
                    emit_LN_stt(i + 1)
                for sl in range(2, 8):
                    emit_kv(sl)
                if i + 1 < NCH:
                    emit_trans(i + 1)

                def vrot(b_):
                    def f(e):
                        cos, sin = cs_sb[b_][:, 0:512], cs_sb[b_][:, 512:1024]
                        ve, vo = V_sb[:, 0:512], V_sb[:, 512:1024]
                        t1 = x_sb[b_][:, 0:512]
                        t2 = x_sb[b_][:, 512:1024]
                        e.tensor_mul(out=t1, in0=ve, in1=cos)
                        e.tensor_mul(out=t2, in0=ve, in1=sin)
                        e.tensor_mul(out=ve, in0=vo, in1=sin)
                        e.tensor_mul(out=vo, in0=vo, in1=cos)
                        e.tensor_sub(out=ve, in0=t1, in1=ve)
                        return e.tensor_add(out=vo, in0=vo, in1=t2)
                    return f
                wv = [("sX", sx_cs[i]), ("sDV", dv_ev[i][5]), ("sDV", dv_xc[i])]
                pl_vrot[i] = E('v', vrot(b), waits=wv, inc='sDV')

                ac1 = E('s', lambda e: e.activation(out=sqdump[:], in_=K_sb[:], func=AF.Square,
                                                    accum_out=sq[:, 0:1]),
                        waits=[("sDV", dv_krot[i]), ("sDV", dv_ev[i][3])], inc='sAC')
                ac_sqv[i] = E('s', lambda e: e.activation(out=sqdump[:], in_=V_sb[:], func=AF.Square,
                                                          accum_out=sq[:, 1:2]),
                              waits=[("sDV", dv_ev[i][7]), ("sDV", pl_vrot[i]), ("sAC", ac1)],
                              inc='sAC')
                ac3 = E('s', lambda e: e.activation(out=lt[:, 0:1], in_=sq[:, 0:1], func=AF.Ln),
                        waits=[("sAC", ac_sqv[i])], inc='sAC')
                ac4 = E('s', lambda e: e.activation(out=lt[:, 1:2], in_=sq[:, 1:2], func=AF.Ln),
                        waits=[("sAC", ac3)], inc='sAC')
                ac5 = E('s', lambda e: e.activation(out=lt[:, 1:2], in_=lt[:, 1:2], func=AF.Copy,
                                                    scale=-0.5),
                        waits=[("sAC", ac4)], inc='sAC')
                ac_rkv[i] = E('s', lambda e: e.activation(out=rkv[:], in_=lt[:, 0:1], func=AF.Exp,
                                                          scale=-0.5, bias=lt[:, 1:2]),
                              waits=[("sAC", ac5)], inc='sAC')
                ac_ksc = E('s', lambda e: e.activation(out=K_sb[:], in_=K_sb[:], func=AF.Copy,
                                                       scale=rkv[:]),
                           waits=[("sAC", ac_rkv[i])], inc='sAC')

                def pacc(i_, b_):
                    def f(e):
                        if i_ == 0:
                            return e.tensor_mul(out=acc[:], in0=K_sb[:], in1=V_sb[:])
                        e.tensor_mul(out=sqdump[:], in0=K_sb[:], in1=V_sb[:])
                        return e.tensor_add(out=acc[:], in0=acc[:], in1=sqdump[:])
                    return f
                dv_P[i] = E('v', pacc(i, b),
                            waits=[("sAC", ac_ksc)], inc='sDV')
                if i + 2 < NCH:
                    emit_xload(i + 2)

            reord = [A_row[0:1, 0:1024:2], A_row[0:1, 1:1024:2],
                     A_row[0:1, 1024:1536], A_row[0:1, 1536:2048]]
            dv_ar = [0] * 4
            for n in range(4):
                w = [("sDV", dv_P[NCH - 1])]
                if n >= 1:
                    w.append(("sDV", dv_ar[n - 1]))
                pe_n = E('t', (lambda n_: lambda e: e.matmul(
                    ps_a[:], ones_c[:], acc[:, n_ * 512:(n_ + 1) * 512],
                    start=True, stop=True))(n), waits=w, inc='sPE')
                dv_ar[n] = E('v', (lambda n_: lambda e: e.tensor_copy(
                    out=reord[n_], in_=ps_a[:]))(n), waits=[("sPE", pe_n)], inc='sDV')

            def acolmm(e):
                r = None
                for fo in range(16):
                    r = e.matmul(ps_ac[:, fo:fo + 1], A_row[0:1, fo * P:(fo + 1) * P],
                                 ones_f[0:1, 0:1], start=True, stop=True)
                return r
            pe_acol = E('t', acolmm, waits=[("sDV", dv_ar[3])], inc='sPE')
            ac_A = E('s', lambda e: e.activation(out=A_col[:], in_=ps_ac[:], func=AF.Copy,
                                                 scale=0.5, bias=0.5),
                     waits=[("sPE", pe_acol)], inc='sAC')

            bar1 = barrier()

        with ExitStack() as p2a:
            e2 = p2a.enter_context
            tmp2 = e2(nc.sbuf_tensor("tmp2", [P, D], F32))
            bq_brow = e2(nc.sbuf_tensor("bq_brow", [1, D], BF16))
            bout_brow = e2(nc.sbuf_tensor("bout_brow", [1, D], BF16))
            row_f = tmp2[0:1, :]
            wqq = [e2(nc.sbuf_tensor(f"wqq{i}", [P, 16 * 512], BF16)) for i in range(2)]
            wrt_sb = e2(nc.sbuf_tensor("wrt_sb", [P, 16 * 16], F32))
            woa_sb = [e2(nc.sbuf_tensor(f"woa{i}", [P, 16 * 512], BF16)) for i in range(2)]
            x2T = [e2(nc.sbuf_tensor(f"x2T{i}", [P, 16 * 512], BF16)) for i in range(2)]
            qbT = e2(nc.sbuf_tensor("qbT", [P, 16 * 512], BF16))
            xn_sb = e2(nc.sbuf_tensor("xn_sb", [P, D], F32))
            xnT_half = e2(nc.sbuf_tensor("xnT_half", [P, 1024], F32))
            s0_buf = [e2(nc.sbuf_tensor(f"s0b{i}", [P, 512], BF16)) for i in range(2)]
            x2b = e2(nc.sbuf_tensor("x2b", [P, D], BF16))
            x2a_sb = [e2(nc.sbuf_tensor(f"x2a{i}", [P, D], F32)) for i in range(1)]
            xnT_bf = x2b
            ps_t2 = e2(nc.psum_tensor("ps_t2", [P, D], BF16))
            ps_tf = e2(nc.psum_tensor("ps_tf", [P, 1024], F32))
            ps_r = e2(nc.psum_tensor("ps_r", [P, 16], F32))
            ps_mm = [e2(nc.psum_tensor(f"ps_mm{i}", [P, 512], F32)) for i in range(2)]

            dv_ln2a = [0] * NCH
            dv_x2ev2 = [0] * NCH
            dv_xn2a = [0] * NCH
            pe_x2T = [0] * NCH
            pe_xnT = [0] * NCH
            ac_probs = [0] * NCH
            ac_rstd2 = [0] * NCH
            st_xn = [0] * NCH
            st_s0_piece = {}
            sx2 = [0] * NCH
            qbank = {0: ("sDV", 0), 1: ("sDV", 0)}

            xq = [x_sb[0], x_sb[1]] + x2a_sb

            def emit_xload2(ci):
                wx = [("sDV", dv_ln2a[ci - 3])] if ci >= 3 else [("sBAR", bar1)]
                sx2[ci] = E('y', (lambda c: lambda e: e.dma_start(
                    out=xq[c % 3][:], in_=x_h[c * P:(c + 1) * P, :]))(ci),
                    waits=wx, inc='sX', n=16)

            def emit_chain(ci):
                b = ci % 2
                xcur = xq[ci % 3]
                tb = (ci // 4) % 2
                j = ci % 4
                mu = mv[:, 2 * b: 2 * b + 1]
                var = mv[:, 2 * b + 1: 2 * b + 2]
                rstd1 = rst[:, 2 * b: 2 * b + 1]
                rstdc = rst[:, 2 * b + 1: 2 * b + 2]
                E('v', (lambda xc_: lambda e: e.bn_stats(
                    out=st6[:, 0:6], in_=xc_[:, 0:512]))(xcur),
                    waits=[("sX", sx2[ci]), ("sBAR", bar1)])
                dv_b = 0
                for q in range(1, 4):
                    dv_b = E('v', (lambda xc_, q_: lambda e: e.bn_stats(
                        out=st6[:, 6 * q_:6 * (q_ + 1)],
                        in_=xc_[:, 512 * q_:512 * (q_ + 1)]))(xcur, q),
                        inc='sDV' if q == 3 else None)
                wagg = [("sDV", dv_b)]
                if ci >= 2:
                    wagg.append(("sAC", ac_rstd2[ci - 2]))
                dv_m = E('v', (lambda b_: lambda e: e.bn_aggr(
                    out=mv[:, 2 * b_: 2 * b_ + 2], in_=st6[:]))(b),
                    waits=wagg, inc='sDV')
                a1 = E('s', (lambda var_: lambda e: e.activation(
                    out=lt[:, 0:1], in_=var_, func=AF.Ln, bias=eps_t[:]))(var),
                    waits=[("sDV", dv_m), ("sBAR", bar1)], inc='sAC')
                a2 = E('s', (lambda var_: lambda e: e.activation(
                    out=lt[:, 1:2], in_=var_, func=AF.Ln, scale=1.0 + EPS,
                    bias=eps2_t[:]))(var),
                    waits=[("sAC", a1)], inc='sAC')
                a3 = E('s', (lambda r1: lambda e: e.activation(
                    out=r1, in_=lt[:, 0:1], func=AF.Exp, scale=-0.5))(rstd1),
                    waits=[("sAC", a2)], inc='sAC')
                ac_rstd2[ci] = E('s', (lambda rc: lambda e: e.activation(
                    out=rc, in_=lt[:, 1:2], func=AF.Exp, scale=-0.5))(rstdc),
                    waits=[("sAC", a3)], inc='sAC')
                E('s', (lambda c, r1: lambda e: e.activation(
                    out=rstd_all[:, c:c + 1], in_=r1, func=AF.Copy, scale=1.0))(ci, rstd1),
                    inc='sAC')
                E('s', (lambda c, mu_: lambda e: e.activation(
                    out=mu_all[:, c:c + 1], in_=mu_, func=AF.Copy, scale=1.0))(ci, mu),
                    inc='sAC')

                wxn = [("sAC", ac_rstd2[ci])]
                if ci >= 1:
                    wxn.append(("sPE", pe_xnT[ci - 1]))
                dv_xn2a[ci] = E('v', (lambda xc_, mu_, r1: lambda e: e.tensor_scalar(
                    out=xn_sb[:], in0=xc_[:], scalar1=mu_, scalar2=r1,
                    op0=ALU.subtract, op1=ALU.mult))(xcur, mu, rstd1),
                    waits=wxn, inc='sDV')
                dv_ln2a[ci] = E('v', (lambda xc_, mu_: lambda e: e.scalar_tensor_tensor(
                    out=tmp2[:], in0=xc_[:], scalar=mu_, in1=gam_b[:],
                    op0=ALU.subtract, op1=ALU.mult))(xcur, mu), inc='sDV')
                wx2 = []
                if ci >= 1:
                    wx2.append(("sPE", pe_x2T[ci - 1]))
                    wx2.append(("sST", st_xn[ci - 1]))
                dv_x2 = E('v', (lambda rc: lambda e: e.scalar_tensor_tensor(
                    out=x2b[:], in0=tmp2[:], scalar=rc, in1=bet_b[:],
                    op0=ALU.mult, op1=ALU.add))(rstdc), waits=wx2, inc='sDV')

                def tr2(e):
                    r = None
                    for fo in range(16):
                        r = e.transpose(ps_t2[:, fo * P:(fo + 1) * P],
                                        x2b[:, fo * P:(fo + 1) * P], idb[:])
                    return r
                wt2 = [("sDV", dv_x2)]
                if ci >= 1:
                    wt2.append(("sDV", dv_x2ev2[ci - 1]))
                pe_x2T[ci] = E('t', tr2, waits=wt2, inc='sPE')
                dv_x2ev2[ci] = E('v', (lambda j_, tb_: lambda e: e.tensor_copy(
                    out=x2T[tb_][:].rearrange("p (k n) -> p k n", k=16)[:, :, j_ * P:(j_ + 1) * P],
                    in_=ps_t2[:].rearrange("p (k n) -> p k n", k=16)))(j, tb),
                    waits=[("sPE", pe_x2T[ci])], inc='sDV')

                dv_bf = 0
                pe_rt = 0
                for h in range(2):
                    def trx(h_):
                        def f(e):
                            r = None
                            for fo in range(8):
                                fa = h_ * 8 + fo
                                r = e.transpose(ps_tf[:, fo * P:(fo + 1) * P],
                                                xn_sb[:, fa * P:(fa + 1) * P], idf[:])
                            return r
                        return f
                    wtx = [("sDV", dv_xn2a[ci])]
                    if h == 1:
                        wtx.append(("sDV", dv_bf))
                    pe_xnT[ci] = E('t', trx(h), waits=wtx, inc='sPE')
                    pe_h = pe_xnT[ci]
                    dv_xh = E('v', lambda e: e.tensor_copy(out=xnT_half[:], in_=ps_tf[:]),
                              waits=[("sPE", pe_h)], inc='sDV')
                    wbf = [("sPE", pe_h)]
                    if ci >= 1:
                        wbf.append(("sST", st_xn[ci - 1]))
                    dv_bf = E('v', (lambda h_: lambda e: e.tensor_copy(
                        out=xnT_bf[:, h_ * 1024:(h_ + 1) * 1024], in_=ps_tf[:]))(h),
                        waits=wbf, inc='sDV')

                    def rtmm(h_):
                        def f(e):
                            r = None
                            for kc in range(8):
                                ka = h_ * 8 + kc
                                r = e.matmul(ps_r[:], xnT_half[:, kc * P:(kc + 1) * P],
                                             wrt_sb[:, ka * 16:(ka + 1) * 16],
                                             start=(ka == 0), stop=(ka == 15))
                            return r
                        return f
                    wrtw = [("sDV", dv_xh), ("sW", w_cst)]
                    if ci >= 1:
                        wrtw.append(("sAC", ac_probs[ci - 1]))
                    pe_rt = E('t', rtmm(h), waits=wrtw, inc='sPE')

                st_xn[ci] = E('y', (lambda c: lambda e: e.dma_start(
                    out=xnTh[:, :, c * P:(c + 1) * P].rearrange("k p n -> p k n"),
                    in_=xnT_bf[:].rearrange("p (k n) -> p k n", k=16)))(ci),
                    waits=[("sDV", dv_bf)], inc='sST', n=16)

                d1 = E('v', lambda e: e.reduce_max(out=negm[:], in_=ps_r[:], axis=AX.X,
                                                   negate=True),
                       waits=[("sPE", pe_rt)], inc='sDV')
                ac_probs[ci] = E('s', lambda e: e.activation(out=probs[:], in_=ps_r[:],
                                                             func=AF.Exp, bias=negm[:]),
                                 waits=[("sDV", d1)], inc='sAC')
                d2 = E('v', lambda e: e.reduce_sum(out=ssum[:, 0:1], in_=probs[:], axis=AX.X),
                       waits=[("sAC", ac_probs[ci])], inc='sDV')
                d3 = E('v', lambda e: e.reciprocal(out=ssum[:, 1:2], in_=ssum[:, 0:1]),
                       waits=[("sDV", d2)], inc='sDV')
                d4 = E('v', lambda e: e.tensor_scalar_mul(probs[:], probs[:], ssum[:, 1:2]),
                       waits=[("sDV", d3)], inc='sDV')
                d5 = E('v', lambda e: e.max(out=m8[:], in_=probs[:]),
                       waits=[("sDV", d4)], inc='sDV')
                d6 = E('v', lambda e: e.tensor_scalar(out=msk[:], in0=probs[:],
                                                      scalar1=m8[:, 1:2], scalar2=None,
                                                      op0=ALU.is_ge),
                       waits=[("sDV", d5)], inc='sDV')
                E('v', (lambda c: lambda e: e.tensor_mul(
                    out=gate_all[:, c * 16:(c + 1) * 16], in0=msk[:], in1=probs[:]))(c=ci),
                    waits=[("sDV", d6)], inc='sDV')
                if ci + 3 < NCH:
                    emit_xload2(ci + 3)

            w_r1 = E('y', lambda e: e.dma_start(out=row_f, in_=bq_row_h),
                     waits=[("sBAR", bar1)], inc='sW', n=16)
            dv_r1 = E('v', lambda e: e.tensor_copy(out=bq_brow[:], in_=row_f),
                      waits=[("sW", w_r1), ("sBAR", bar1)], inc='sDV')
            w_r2 = E('y', lambda e: e.dma_start(out=row_f, in_=bout_h),
                     waits=[("sDV", dv_r1)], inc='sW', n=16)
            w_r2d = E('v', lambda e: e.tensor_copy(out=bout_brow[:], in_=row_f),
                      waits=[("sW", w_r2)], inc='sDV')
            pe_woa_used = {0: 0, 1: 0}
            pe_wq_used = {0: 0, 1: 0}
            w_cst = E('y', lambda e: e.dma_start(
                out=wrt_sb[:].rearrange("p (k n) -> p k n", k=16),
                in_=wrt_h.rearrange("(k p) n -> p k n", p=P)),
                waits=[("sBAR", bar1)], inc='sW', n=16)
            for ci in range(3):
                emit_xload2(ci)
            for ci in range(4):
                emit_chain(ci)

            for t in range(NT):
                tb = t % 2
                next_chains = [4 * (t + 1) + k for k in range(4)] if t + 1 < NT else []
                w_wqq = [0] * 4
                for q in range(2):
                    w_wqq[q] = E('y', (lambda q_, b_: lambda e: e.dma_start(
                        out=wqq[b_][:].rearrange("p (k n) -> p k n", k=16),
                        in_=wqbf[:, :, q_ * 512:(q_ + 1) * 512].rearrange("k p n -> p k n")))(q, q % 2),
                        waits=[("sG", g_wq), ("sPE", pe_wq_used.get(q % 2, 0))] +
                              ([("sBAR", bar1)] if t == 0 else []),
                        inc='sW', n=16)
                for mc in range(16):
                    bk = mc % 2
                    qq = mc // 4
                    if mc % 4 == 0 and qq >= 2:
                        w_wqq[qq] = E('y', (lambda q_, b_: lambda e: e.dma_start(
                            out=wqq[b_][:].rearrange("p (k n) -> p k n", k=16),
                            in_=wqbf[:, :, q_ * 512:(q_ + 1) * 512].rearrange("k p n -> p k n")))(qq, qq % 2),
                            waits=[("sPE", pe_wq_used.get(qq % 2, 0))], inc='sW', n=16)
                    w = [("sDV", dv_x2ev2[4 * t + 3]), ("sW", w_wqq[qq]), qbank[bk]]

                    def qmm(mc_, tb_, qb_):
                        def f(e):
                            for kc in range(16):
                                e.matmul(ps_mm[mc_ % 2][:],
                                         wqq[qb_][:, kc * 512 + (mc_ % 4) * P: kc * 512 + (mc_ % 4 + 1) * P],
                                         x2T[tb_][:, kc * 512:(kc + 1) * 512],
                                         start=(kc == 0), stop=False)
                            for n4 in range(3):
                                e.matmul(ps_mm[mc_ % 2][:, n4 * P:(n4 + 1) * P],
                                         bq_brow[0:1, mc_ * P:(mc_ + 1) * P],
                                         ones_b128[0:1, :], start=False, stop=False)
                            return e.matmul(ps_mm[mc_ % 2][:, 3 * P:4 * P],
                                            bq_brow[0:1, mc_ * P:(mc_ + 1) * P],
                                            ones_b128[0:1, :], start=False, stop=True)
                        return f
                    pe_q = E('t', qmm(mc, tb, qq % 2), waits=w, inc='sPE')
                    pe_wq_used[qq % 2] = pe_q
                    ac_q = E('s', (lambda mc_: lambda e: e.activation(
                        out=qbT[:, mc_ * 512:(mc_ + 1) * 512], in_=ps_mm[mc_ % 2][:],
                        func=AF.Copy, scale=A_col[:, mc_:mc_ + 1]))(mc),
                        waits=[("sPE", pe_q), ("sAC", ac_A)], inc='sAC')
                    qbank[bk] = ("sAC", ac_q)
                    if mc == 7 and next_chains:
                        emit_chain(next_chains[0])
                    if mc == 15 and next_chains:
                        emit_chain(next_chains[1])
                ac_q15 = qbank[1]

                w_woa_t = [0] * 4
                for nch in range(2):
                    wb = nch % 2
                    wl = [("sG", g_woa), ("sPE", pe_woa_used.get(wb, 0)), ("sBAR", bar1)]
                    w_woa_t[nch] = E('y', (lambda n_, b_: lambda e: e.dma_start(
                        out=woa_sb[b_][:].rearrange("p (k n) -> p k n", k=16),
                        in_=woabf[:, :, n_ * 512:(n_ + 1) * 512].rearrange("k p n -> p k n")))(nch, wb),
                        waits=wl, inc='sW', n=16)
                for nch in range(4):
                    wb = nch % 2
                    if nch >= 2:
                        wl = [("sG", g_woa), ("sPE", pe_woa_used.get(wb, 0))]
                        w_woa_t[nch] = E('y', (lambda n_, b_: lambda e: e.dma_start(
                            out=woa_sb[b_][:].rearrange("p (k n) -> p k n", k=16),
                            in_=woabf[:, :, n_ * 512:(n_ + 1) * 512].rearrange("k p n -> p k n")))(nch, wb),
                            waits=wl, inc='sW', n=16)
                    w_woa = w_woa_t[nch]
                    for j in range(4):
                        bk = j % 2
                        w = [("sW", w_woa), ac_q15, qbank[bk]]

                        def wmm(n_, j_):
                            def f(e):
                                for kc in range(16):
                                    e.matmul(ps_mm[j_ % 2][:],
                                             qbT[:, kc * 512 + j_ * P: kc * 512 + (j_ + 1) * P],
                                             woa_sb[n_ % 2][:, kc * 512:(kc + 1) * 512],
                                             start=(kc == 0), stop=False)
                                return e.matmul(ps_mm[j_ % 2][:],
                                                ones_b128[0:1, :],
                                                bout_brow[0:1, n_ * 512:(n_ + 1) * 512],
                                                start=False, stop=True)
                            return f
                        pe_wj = E('t', wmm(nch, j), waits=w, inc='sPE')
                        pe_woa_used[wb] = pe_wj
                        sb = (nch * 4 + j) % 2
                        pidx = nch * 4 + j
                        ws0 = [("sPE", pe_wj)]
                        if pidx >= 2:
                            ws0.append(("sST", st_s0_piece.get((t, pidx - 2), 0)))
                        elif t >= 1:
                            ws0.append(("sST", st_s0_piece.get((t - 1, 14 + pidx), 0)))
                        ac_s0 = E('s', (lambda j_, b_: lambda e: e.activation(
                            out=s0_buf[b_][:], in_=ps_mm[j_ % 2][:], func=AF.Copy,
                            scale=1.0))(j, sb),
                            waits=ws0, inc='sAC')
                        qbank[bk] = ("sAC", ac_s0)
                        st_s0_piece[(t, pidx)] = E('y', (lambda n_, c, b_: lambda e: e.dma_start(
                            out=s0h[c][:, n_ * 512:(n_ + 1) * 512], in_=s0_buf[b_][:]))(
                                nch, 4 * t + j, sb),
                            waits=[("sAC", ac_s0)], inc='sST', n=16)
                    if nch < 2 and next_chains:
                        emit_chain(next_chains[2 + nch])

            st_2a_total = now('sST')
            bar2 = barrier()

        with ExitStack() as p2b:
            e3 = p2b.enter_context
            xnT_sb = e3(nc.sbuf_tensor("xnT_sb", [P, 16 * 512], BF16))
            s0_sb = [e3(nc.sbuf_tensor(f"s0s{i}", [P, D], BF16)) for i in range(2)]
            acc2 = e3(nc.sbuf_tensor("acc2", [P, 4 * D], F32))
            wup_sb = [e3(nc.sbuf_tensor(f"wup{i}", [P, 16 * 512], BF16)) for i in range(NUPB)]
            wdn_sb = [e3(nc.sbuf_tensor(f"wdn{i}", [P, 8 * D], BF16)) for i in range(2)]
            mid = [e3(nc.sbuf_tensor(f"mid{i}", [P, 8 * 512], BF16)) for i in range(2)]
            g_buf = e3(nc.sbuf_tensor("g_buf", [P, D], F32))
            ps_up = [e3(nc.psum_tensor(f"ps_up{i}", [P, 512], F32)) for i in range(2)]
            ps_dn = [e3(nc.psum_tensor(f"ps_dn{i}", [P, 1024], F32)) for i in range(2)]

            pe_up_piece = {}
            ac_gelu_gm = {}
            dv_dnev_gf = {}
            ac_silu = {}
            dv_comb = {}
            out_free = {}
            wdn_user = {0: 0, 1: 0}
            mid_user = {0: 0, 1: 0}
            dn_load_val = {}
            pe_up_last = [0] * NT
            w2_up = {}
            dv3_prev = 0

            def emit_wup_load(t, ee, m2):
                gp = (t * NE + ee) * 2 + m2
                pb = gp % NUPB
                wu = [("sG", g_wup[ee]), ("sPE", pe_up_piece.get(gp - NUPB, 0))]
                w2_up[gp] = E('s', (lambda e_, m_, b_: lambda e: e.dma_start(
                    out=wup_sb[b_][:].rearrange("p (k n) -> p k n", k=16),
                    in_=wupbf[e_][:, :, m_ * 512:(m_ + 1) * 512].rearrange("k p n -> p k n")))(ee, m2, pb),
                    waits=wu, inc='sW2', n=16)

            for t in range(NT):
                wxt = [("sST", st_xn[4 * t + 3])]
                if t >= 1:
                    wxt.append(("sPE", pe_up_last[t - 1]))
                else:
                    wxt.append(("sBAR", bar2))
                w_xnT = E('y', (lambda t_: lambda e: e.dma_start(
                    out=xnT_sb[:].rearrange("p (k n) -> p k n", k=16),
                    in_=xnTh[:, :, t_ * 512:(t_ + 1) * 512].rearrange("k p n -> p k n")))(t),
                    waits=wxt, inc='sW', n=16)

                if t == 0:
                    emit_wup_load(0, 0, 0)

                def emit_dn_unit(t_, unit_ee):
                    if unit_ee < 0:
                        u = t_ * 17
                        wld = dn_load_val[(t_, 1)]
                        mid_rd = ("sAC", ac_gelu_gm[(t_ * NE + 1) * 8 + 7])
                    else:
                        u = t_ * 17 + 1 + (unit_ee - 2)
                        wld = dn_load_val[(t_, unit_ee)]
                        mid_rd = ("sAC", ac_gelu_gm[(t_ * NE + unit_ee) * 8 + 7])
                    last_pe = 0
                    for j in range(4):
                        ci = 4 * t_ + j
                        for nh in range(2):
                            q = 2 * j + nh
                            uf = u * 8 + q
                            r = q % 2
                            wdn_w = [("sW", wld),
                                     ("sDV", dv_dnev_gf.get(uf - 2, 0)), mid_rd]

                            if unit_ee < 0:
                                def dnmm(j_, nh_, r_):
                                    def f(e):
                                        rr = None
                                        for ei2 in range(2):
                                            for kc in range(8):
                                                for n2 in range(2):
                                                    rr = e.matmul(
                                                        ps_dn[r_][:, n2 * 512:(n2 + 1) * 512],
                                                        mid[ei2][:, kc * 512 + j_ * P: kc * 512 + (j_ + 1) * P],
                                                        wdn_sb[ei2][:, kc * D + nh_ * 1024 + n2 * 512: kc * D + nh_ * 1024 + (n2 + 1) * 512],
                                                        start=(ei2 == 0 and kc == 0),
                                                        stop=(ei2 == 1 and kc == 7),
                                                        skip_group_check=True)
                                        return rr
                                    return f
                                pe_f = E('t', dnmm(j, nh, r), waits=wdn_w, inc='sPE')
                                wev = [("sPE", pe_f),
                                       ("sAC", ac_silu.get((t_ - 1, j), 0)),
                                       ("sDV", dv_comb.get((t_ - 1, j), 0))]

                                def dnev(j_, nh_, r_):
                                    def f(e):
                                        dst = acc2[:, j_ * D + nh_ * 1024: j_ * D + (nh_ + 1) * 1024]
                                        return e.tensor_copy(out=dst, in_=ps_dn[r_][:])
                                    return f
                                dv_dnev_gf[uf] = E('v', dnev(j, nh, r),
                                                   waits=wev, inc='sDV')
                            else:
                                def dnmm(e_, j_, nh_, r_):
                                    def f(e):
                                        rr = None
                                        eb = e_ % 2
                                        for kc in range(8):
                                            for n2 in range(2):
                                                rr = e.matmul(
                                                    ps_dn[r_][:, n2 * 512:(n2 + 1) * 512],
                                                    mid[eb][:, kc * 512 + j_ * P: kc * 512 + (j_ + 1) * P],
                                                    wdn_sb[eb][:, kc * D + nh_ * 1024 + n2 * 512: kc * D + nh_ * 1024 + (n2 + 1) * 512],
                                                    start=(kc == 0), stop=(kc == 7),
                                                    skip_group_check=True)
                                        return rr
                                    return f
                                pe_f = E('t', dnmm(unit_ee, j, nh, r),
                                         waits=wdn_w, inc='sPE')
                                wev = [("sPE", pe_f)]

                                def dnev(e_, c_, j_, nh_, r_):
                                    def f(e):
                                        dst = acc2[:, j_ * D + nh_ * 1024: j_ * D + (nh_ + 1) * 1024]
                                        return e.scalar_tensor_tensor(
                                            out=dst, in0=ps_dn[r_][:],
                                            scalar=gate_all[:, c_ * 16 + (e_ - 2): c_ * 16 + (e_ - 2) + 1],
                                            in1=dst, op0=ALU.mult, op1=ALU.add)
                                    return f
                                dv_dnev_gf[uf] = E('v', dnev(unit_ee, ci, j, nh, r),
                                                   waits=wev, inc='sDV')
                            last_pe = pe_f
                    if unit_ee < 0:
                        wdn_user[0] = last_pe
                        wdn_user[1] = last_pe
                        mid_user[0] = last_pe
                        mid_user[1] = last_pe
                    else:
                        wdn_user[unit_ee % 2] = last_pe
                        mid_user[unit_ee % 2] = last_pe

                for ee in range(NE):
                    ei = ee % 2
                    ge = t * NE + ee

                    if ee == 2:
                        emit_dn_unit(t, -1)
                    elif ee >= 3:
                        emit_dn_unit(t, ee - 1)

                    wd = [("sG", g_wdn[ee]), ("sPE", wdn_user.get(ee % 2, 0))]
                    dn_load_val[(t, ee)] = E('y', (lambda e_, b_: lambda e: e.dma_start(
                        out=wdn_sb[b_][:].rearrange("p (k n) -> p k n", k=8),
                        in_=wdnbf[e_].rearrange("k p n -> p k n")))(ee, ee % 2),
                        waits=wd, inc='sW', n=16)

                    for m2 in range(2):
                        gp = ge * 2 + m2
                        pb = gp % NUPB
                        nt_, ne_, nm_ = t, ee, m2 + 1
                        if nm_ >= 2:
                            nm_ -= 2
                            ne_ += 1
                            if ne_ >= NE:
                                ne_ = 0
                                nt_ += 1
                        if nt_ < NT:
                            emit_wup_load(nt_, ne_, nm_)
                        for mc in range(4):
                            ma = m2 * 4 + mc
                            gm = ge * 8 + ma
                            ub = gm % 2
                            wmm_ = [("sW2", w2_up[gp]), ("sW", w_xnT),
                                    ("sAC", ac_gelu_gm.get(gm - 2, 0))]

                            def upmm(mc_, b_, ub_):
                                def f(e):
                                    r = None
                                    for kc in range(16):
                                        r = e.matmul(
                                            ps_up[ub_][:],
                                            wup_sb[b_][:, kc * 512 + mc_ * P: kc * 512 + (mc_ + 1) * P],
                                            xnT_sb[:, kc * 512:(kc + 1) * 512],
                                            start=(kc == 0), stop=(kc == 15))
                                    return r
                                return f
                            pe_this = E('t', upmm(mc, pb, ub), waits=wmm_, inc='sPE')
                            pe_up_piece[gp] = pe_this
                            if ee == NE - 1 and m2 == 1:
                                pe_up_last[t] = pe_this

                            wge = [("sPE", pe_this), ("sPE", mid_user.get(ei, 0))]
                            ac_gelu_gm[gm] = E('s', (lambda ma_, ei_, ub_: lambda e: e.activation(
                                out=mid[ei_][:, ma_ * 512:(ma_ + 1) * 512],
                                in_=ps_up[ub_][:], func=AF.Gelu))(ma, ei, ub),
                                waits=wge, inc='sAC')

                emit_dn_unit(t, NE - 1)

                g_x = [0] * 4
                g_s0 = [0] * 4

                def emit_comb_loads(j):
                    ci = 4 * t + j
                    b = j % 2
                    if t == 0 and j < 2:
                        wxl = [("sBAR", bar2)]
                    elif j < 2:
                        wxl = [("sOUT", out_free.get((t - 1, j + 2), 0))]
                    else:
                        wxl = [("sOUT", out_free.get((t, j - 2), 0))]
                    g_x[j] = E('g', (lambda c, b_: lambda e: e.dma_start(
                        out=x_sb[b_][:], in_=x_h[c * P:(c + 1) * P, :]))(ci, b),
                        waits=wxl, inc='sG', n=16)
                    ws0l = [("sST", st_2a_total)]
                    prevc = dv_comb.get((t - 1, j + 2) if j < 2 else (t, j - 2), 0)
                    if prevc:
                        ws0l.append(("sDV", prevc))
                    g_s0[j] = E('g', (lambda c, b_: lambda e: e.dma_start(
                        out=s0_sb[b_][:], in_=s0h[c]))(ci, b),
                        waits=ws0l, inc='sG', n=16)

                emit_comb_loads(0)
                emit_comb_loads(1)
                for j in range(4):
                    ci = 4 * t + j
                    b = j % 2
                    wsl = [("sDV", dv_dnev_gf[(t * 17 + 16) * 8 + 2 * j + 1]),
                           ("sDV", dv3_prev),
                           ("sOUT", out_free.get((t - 1, j + 2) if j < 2 else (t, j - 2), 0))]
                    if j >= 1:
                        wsl.append(("sDV", dv_comb.get((t, j - 1), 0)))
                    ac_silu[(t, j)] = E('s', (lambda j_: lambda e: e.activation(
                        out=g_buf[:], in_=acc2[:, j_ * D:(j_ + 1) * D], func=AF.Silu))(j),
                        waits=wsl, inc='sAC')
                    dv1 = E('v', (lambda j_, b_, c: lambda e: e.tensor_scalar(
                        out=acc2[:, j_ * D:(j_ + 1) * D], in0=x_sb[b_][:],
                        scalar1=mu_all[:, c:c + 1], scalar2=rstd_all[:, c:c + 1],
                        op0=ALU.subtract, op1=ALU.mult))(j, b, ci),
                        waits=[("sG", g_x[j]), ("sAC", ac_silu[(t, j)])], inc='sDV')
                    dv2 = E('v', (lambda j_, b_: lambda e: e.tensor_add(
                        out=acc2[:, j_ * D:(j_ + 1) * D], in0=acc2[:, j_ * D:(j_ + 1) * D],
                        in1=s0_sb[b_][:]))(j, b),
                        waits=[("sG", g_s0[j])], inc='sDV')
                    dv3 = E('v', (lambda j_: lambda e: e.tensor_mul(
                        out=g_buf[:], in0=acc2[:, j_ * D:(j_ + 1) * D], in1=g_buf[:]))(j),
                        waits=[("sDV", dv2), ("sAC", ac_silu[(t, j)])], inc='sDV')
                    dv3_prev = dv3
                    dv_comb[(t, j)] = E('v', (lambda b_: lambda e: e.tensor_add(
                        out=x_sb[b_][:], in0=g_buf[:], in1=x_sb[b_][:]))(b),
                        waits=[("sDV", dv3)], inc='sDV')
                    out_free[(t, j)] = E('g', (lambda c, b_: lambda e: e.dma_start(
                        out=y_h[c * P:(c + 1) * P, :], in_=x_sb[b_][:]))(ci, b),
                        waits=[("sDV", dv_comb[(t, j)])], inc='sOUT', n=16)
                    if j + 2 < 4:
                        emit_comb_loads(j + 2)

            E('g', lambda e: e.memset(bar_t[0:1, 3:4], 0.0),
                waits=[("sOUT", now('sOUT'))])

        with nc.Block() as block:
            @block.tensor
            def _(eng):
                sch.replay('t', eng, sems)

            @block.vector
            def _(eng):
                sch.replay('v', eng, sems)

            @block.scalar
            def _(eng):
                sch.replay('s', eng, sems)

            @block.gpsimd
            def _(eng):
                sch.replay('g', eng, sems)

            @block.sync
            def _(eng):
                sch.replay('y', eng, sems)

    return nc


def _host_inputs(inputs):
    W_in = np.ascontiguousarray(np.asarray(inputs["L0_attn_W_in"], dtype=np.float32))
    b_in = np.asarray(inputs["L0_attn_b_in"], dtype=np.float32)
    perm = np.concatenate([np.arange(0, 1024, 2), np.arange(1, 1024, 2),
                           np.arange(1024, 2048)])
    wq = np.ascontiguousarray(W_in[:, 0:2048])
    wkv = np.ascontiguousarray(
        np.concatenate([W_in[:, 2048:4096][:, perm], W_in[:, 4096:6144][:, perm]], axis=1))
    bq_row = np.ascontiguousarray(b_in[0:2048][None, :])
    bkv_row = np.ascontiguousarray(
        np.concatenate([b_in[2048:4096][perm], b_in[4096:6144][perm]])[None, :])
    gam_row = np.ascontiguousarray(np.asarray(inputs["L0_attn_gamma"], np.float32)[None, :])
    bet_row = np.ascontiguousarray(np.asarray(inputs["L0_attn_beta"], np.float32)[None, :])
    bout_row = np.ascontiguousarray(np.asarray(inputs["L0_attn_b_out"], np.float32)[None, :])
    wout = np.ascontiguousarray(np.asarray(inputs["L0_attn_W_out"], np.float32))
    wrt = np.ascontiguousarray(np.asarray(inputs["L0_ffn_W_router"], np.float32))
    wup = np.ascontiguousarray(np.concatenate(
        [np.asarray(inputs["L0_ffn_Wsh_up"], np.float32),
         np.asarray(inputs["L0_ffn_Wrt_up"], np.float32)], axis=0))
    wdn = np.ascontiguousarray(np.concatenate(
        [np.asarray(inputs["L0_ffn_Wsh_down"], np.float32),
         np.asarray(inputs["L0_ffn_Wrt_down"], np.float32)], axis=0))

    inv = 1.0 / (10000.0 ** (np.arange(0, 1024, 2, dtype=np.float64) / 1024.0))
    fr = np.arange(S, dtype=np.float64)[:, None] * inv[None, :]
    cs = np.concatenate([np.cos(fr), np.sin(fr)], axis=1).astype(BF)
    idf = np.eye(P, dtype=np.float32)
    idb = np.eye(P, dtype=BF)
    return dict(wq=wq, wkv=wkv, bq_row=bq_row, bkv_row=bkv_row, gam_row=gam_row,
                bet_row=bet_row, bout_row=bout_row, wout=wout, wrt=wrt,
                wup=wup, wdn=wdn, cs=np.ascontiguousarray(cs), idf=idf, idb=idb)


def kernel(**inputs):
    X = np.asarray(inputs["L0_attn_X"], dtype=np.float32)
    shared = _host_inputs(inputs)
    nc = build_nc()
    in_maps = [dict(x=np.ascontiguousarray(X[c]), **shared) for c in range(8)]
    res = run_bass_kernel_spmd(nc, in_maps, core_ids=list(range(8)))
    out = np.stack([np.asarray(r["y"]) for r in res.results], axis=0)
    return out.astype(np.float32)
